# version 05
# baseline (speedup 1.0000x reference)
from contextlib import ExitStack

import numpy as np
import ml_dtypes

import concourse.bass as bass
import concourse.mybir as mybir
from concourse.bass_utils import run_bass_kernel_spmd

F32 = mybir.dt.float32
BF16 = mybir.dt.bfloat16
U8 = mybir.dt.uint8
AF = mybir.ActivationFunctionType
ALU = mybir.AluOpType
AX = mybir.AxisListType
bfdt = ml_dtypes.bfloat16

T = 2048
C = 1024
H1 = 4096
TS = 1024
NTILE = 16
EPS = 1e-5

ENGS = "tvagy"


class Prog:
    def __init__(self, nc):
        self.nc = nc
        self.q = {e: [] for e in ENGS}
        self.sems = {}
        self.cnt = {}
        self.waited = {e: {} for e in ENGS}

    def mksem(self, ctx, *names):
        for n in names:
            self.sems[n] = ctx.enter_context(self.nc.semaphore(n))
            self.cnt[n] = 0

    def op(self, e, f, waits=(), inc=None, inc_n=1):
        tok = None
        if inc is not None:
            self.cnt[inc] += inc_n
            tok = (inc, self.cnt[inc])
        merged = {}
        for w in waits:
            if w is None:
                continue
            s, v = w
            if v > merged.get(s, -1):
                merged[s] = v
        wl = []
        for s, v in merged.items():
            if self.waited[e].get(s, -1) >= v:
                continue
            self.waited[e][s] = v
            wl.append((s, v))
        sems = self.sems

        def run(eng, wl=wl, f=f, inc=inc, inc_n=inc_n):
            for s, v in wl:
                eng.wait_ge(sems[s], v)
            inst = f(eng)
            if inc is not None:
                inst.then_inc(sems[inc], inc_n)
        self.q[e].append(run)
        return tok

    def dma(self, e, sem, out, in_, waits=()):
        return self.op(e, lambda eng: eng.dma_start(out=out, in_=in_),
                       waits=waits, inc=sem, inc_n=16)


class Arena:
    def __init__(self, t, ncols, name):
        self.t = t
        self.n = ncols
        self.off = 0
        self.stack = []
        self.hwm = 0
        self.name = name

    def alloc(self, cols, align=8):
        self.off = (self.off + align - 1) // align * align
        o = self.off
        self.off += cols
        self.hwm = max(self.hwm, self.off)
        assert self.off <= self.n, f"{self.name} overflow {self.off}>{self.n}"
        return o

    def push(self):
        self.stack.append(self.off)

    def pop(self):
        self.off = self.stack.pop()


def build_program(stage="full", dbg=False):
    nc = bass.Bass("TRN2", target_bir_lowering=False)

    d_x = nc.dram_tensor("x_in", [T, C], F32, kind="ExternalInput")
    d_w = {n: nc.dram_tensor(n, [C, C], F32, kind="ExternalInput")
           for n in ("wqt", "wkt", "wvt", "wpt")}
    d_bqc = nc.dram_tensor("bq_c", [128, 8], F32, kind="ExternalInput")
    d_bkc = nc.dram_tensor("bk_c", [128, 8], F32, kind="ExternalInput")
    d_brow = nc.dram_tensor("b_row", [2, C], F32, kind="ExternalInput")
    d_lam = nc.dram_tensor("lam", [8, 256], F32, kind="ExternalInput")
    d_k0 = nc.dram_tensor("k0", [2 * C, H1], U8, kind="ExternalInput")
    d_k1 = nc.dram_tensor("k1", [H1, H1], U8, kind="ExternalInput")
    d_k2 = nc.dram_tensor("k2", [H1, C], U8, kind="ExternalInput")
    d_s = nc.dram_tensor("s_r", [128, 3], F32, kind="ExternalInput")
    d_idn = nc.dram_tensor("idn", [128, 128], F32, kind="ExternalInput")
    d_out = nc.dram_tensor("out", [T, C], F32, kind="ExternalOutput")

    s_w0d = nc.dram_tensor("s_w0d", [32, C, 128], BF16)
    s_w1x = nc.dram_tensor("s_w1x", [32, H1, 128], BF16)
    s_w2x = nc.dram_tensor("s_w2x", [H1, C], BF16)
    s_wt = nc.dram_tensor("s_wt", [4, C, C], BF16)
    s_q = nc.dram_tensor("s_q", [C, T], BF16)
    s_kgi = nc.dram_tensor("s_kgi", [C, T], BF16)
    s_kgo = nc.dram_tensor("s_kgo", [2 * C, T], BF16)
    s_vgi = nc.dram_tensor("s_vgi", [T, C], BF16)
    s_vgo = nc.dram_tensor("s_vgo", [2 * T, C], BF16)

    if dbg:
        d_dbgq = nc.dram_tensor("dbg_q", [C, T], BF16, kind="ExternalOutput")
        d_dbgk = nc.dram_tensor("dbg_kg", [2 * C, T], BF16, kind="ExternalOutput")
        d_dbgv = nc.dram_tensor("dbg_vg", [2 * T, C], BF16, kind="ExternalOutput")
        d_dbga = nc.dram_tensor("dbg_afm", [128, 2 * T], BF16, kind="ExternalOutput")

    with ExitStack() as ctx:
        NF, NB, NU = 23296, 58600, 2048
        AFT = ctx.enter_context(nc.sbuf_tensor("AF", [128, NF], F32))
        ABT = ctx.enter_context(nc.sbuf_tensor("AB", [128, NB], BF16))
        AUT = ctx.enter_context(nc.sbuf_tensor("AU", [128, NU], U8))
        PSF = ctx.enter_context(nc.psum_tensor("PSF", [128, 4096], F32))
        af = Arena(AFT, NF, "AF")
        ab = Arena(ABT, NB, "AB")

        P = Prog(nc)
        P.mksem(ctx, "t", "v", "a", "g", "cc", "sx", "ssm", "su8",
                "sc0", "sc1", "sc1b", "sc2", "sc2b", "scw", "wsa", "wsb",
                "w0a", "w0b", "w1a", "w1b", "w2a", "w2b",
                "sqw", "sga", "sgb", "swp", "aa0", "aa1",
                "gm0", "gm1", "grc", "gt0", "gt1", "go", "gdb", "av0", "av1")

        def bank(b, p0=0, p1=128, c0=0, c1=512):
            return PSF[p0:p1, b * 512 + c0:b * 512 + c1]

        def region(b, w=1024, p0=0, p1=128):
            return PSF[p0:p1, b * 512:b * 512 + w]

        psum_free = {}
        x_tok = {}

        o_x = af.alloc(NTILE * C)
        o_idn = af.alloc(128)
        o_sc = af.alloc(16)
        o_one8 = af.alloc(8)
        o_onef = af.alloc(128)
        o_bq = af.alloc(8)
        o_bk = af.alloc(8)
        o_lam = af.alloc(264)
        o_stat = af.alloc(64)
        o_xst = af.alloc(2 * C)
        o_wst = af.alloc(2 * C)
        o_m8 = af.alloc(8)
        o_mbuf = af.alloc(32)
        o_row = af.alloc(2048)
        o_rcp = o_row
        o_rbc = o_row + 1024

        def afs(off, p0, p1, c0, c1):
            return AFT[p0:p1, off + c0:off + c1]

        def abs_(off, p0, p1, c0, c1):
            return ABT[p0:p1, off + c0:off + c1]

        def xt(i):
            return afs(o_x, 0, 128, i * C, (i + 1) * C)

        IDN = afs(o_idn, 0, 128, 0, 128)
        S0D = afs(o_sc, 0, 128, 0, 1)
        S1R = afs(o_sc, 0, 128, 1, 2)
        S2H = afs(o_sc, 0, 128, 2, 3)
        LVN = afs(o_sc, 0, 128, 3, 4)
        SRAW = afs(o_sc, 0, 128, 4, 7)
        ONESF = afs(o_onef, 0, 1, 0, 128)
        ONES8 = afs(o_one8, 0, 8, 0, 1)

        o_cvt = ab.alloc(1024)
        o_cvt2 = ab.alloc(1024)
        o_oneb = ab.alloc(64)
        o_rb16 = ab.alloc(1024)
        o_bvb = ab.alloc(C)
        o_bpb = ab.alloc(C)
        CVT = abs_(o_cvt, 0, 128, 0, 1024)
        CVT2 = abs_(o_cvt2, 0, 128, 0, 1024)
        ONESB = abs_(o_oneb, 0, 1, 0, 64)
        RB16A = abs_(o_rb16, 0, 1, 0, 512)
        RB16B = abs_(o_rb16, 0, 1, 512, 1024)
        BVB = abs_(o_bvb, 0, 128, 0, C)
        BPB = abs_(o_bpb, 0, 128, 0, C)

        for i in range(NTILE):
            P.dma("y", "sx", xt(i), d_x[i * 128:(i + 1) * 128, :])
        t_x = ("sx", 16 * NTILE)
        for i in range(NTILE):
            x_tok[i] = t_x
        P.dma("y", "ssm", IDN, d_idn[:, :])
        P.dma("y", "ssm", SRAW, d_s[:, :])
        P.dma("y", "ssm", afs(o_bq, 0, 128, 0, 8), d_bqc[:, :])
        P.dma("y", "ssm", afs(o_bk, 0, 128, 0, 8), d_bkc[:, :])
        P.dma("y", "ssm", afs(o_row, 0, 1, 0, 2048), d_brow[:, :])
        P.dma("y", "ssm", afs(o_lam, 0, 8, 0, 256), d_lam[:, :])
        t_sm = ("ssm", 96)
        t_idn = t_s = t_bq = t_bk = t_row = t_lam = t_sm

        tv_s0 = P.op("v", lambda e: e.tensor_scalar_mul(S0D, SRAW[:, 0:1], 2.0),
                     waits=[t_s], inc="v")
        tv_s1 = P.op("v", lambda e: e.tensor_copy(out=S1R, in_=SRAW[:, 1:2]),
                     inc="v")
        tv_s2 = P.op("v", lambda e: e.tensor_scalar_mul(S2H, SRAW[:, 2:3], 0.5),
                     inc="v")
        tv_bq = P.op("v", lambda e: e.tensor_scalar_mul(
            afs(o_bq, 0, 128, 0, 8), afs(o_bq, 0, 128, 0, 8), 0.125),
            waits=[t_bq], inc="v")
        tv_onef = P.op("v", lambda e: e.memset(ONESF, 1.0), inc="v")
        tv_oneb = P.op("v", lambda e: e.memset(ONESB, 1.0), inc="v")
        tv_one8 = P.op("v", lambda e: e.memset(ONES8, 1.0), inc="v")

        te = None
        for ri, dst in ((0, BVB), (1, BPB)):
            for h in range(2):
                tmm = P.op("t", lambda e, ri=ri, h=h: e.matmul(
                    bank(7), ONESF,
                    afs(o_row, 0, 1, ri * C + h * 512, ri * C + (h + 1) * 512),
                    start=True, stop=True),
                    waits=[t_row, tv_onef, psum_free.get(7)], inc="t")
                te = P.op("a", lambda e, dst=dst, h=h: e.copy(
                    out=dst[:, h * 512:(h + 1) * 512], in_=bank(7)),
                    waits=[tmm], inc="a")
                psum_free[7] = te
        t_bias = te

        wst_free = {}
        lastwc = None
        for wi, dwn in enumerate(("wqt", "wkt", "wvt", "wpt")):
            for ic in range(8):
                b = ic % 2
                st = afs(o_wst, 0, 128, b * C, (b + 1) * C)
                ta = P.dma("y", "wsa" if b == 0 else "wsb", st,
                           d_w[dwn][ic * 128:(ic + 1) * 128, :],
                           waits=[wst_free.get(b)])
                tcv = P.op("a", lambda e, st=st: e.copy(out=CVT2, in_=st),
                           waits=[ta, lastwc], inc="a")
                wst_free[b] = tcv
                lastwc = P.dma("y", "scw", s_wt[wi, ic * 128:(ic + 1) * 128, :],
                               CVT2, waits=[tcv])
        t_wtrdy = ("scw", P.cnt["scw"])
        U8A = AUT[:, 0:1024]
        U8B = AUT[:, 1024:2048]
        lastu = None
        last_sc = None
        for ic in range(8):
            for g in range(4):
                ta = P.dma("y", "su8", U8A,
                           d_k0[ic * 128:(ic + 1) * 128, g * 1024:(g + 1) * 1024],
                           waits=[lastu])
                tb = P.dma("y", "su8", U8B,
                           d_k0[C + ic * 128:C + (ic + 1) * 128,
                                g * 1024:(g + 1) * 1024], waits=[lastu])
                lastu = P.op("g", lambda e: e.tensor_tensor(CVT, U8A, U8B,
                                                            ALU.subtract),
                             waits=[ta, tb, last_sc], inc="g")
                src = CVT.rearrange("p (h j) -> p h j", h=8)
                dst = s_w0d[g * 8:(g + 1) * 8, ic * 128:(ic + 1) * 128,
                            :].rearrange("h p j -> p h j")
                last_sc = P.dma("y", "sc0", dst, src, waits=[lastu])
        t_w0rdy = ("sc0", P.cnt["sc0"])
        conv_hist = {0: lastu, 1: lastu}
        sc_hist = {0: last_sc, 1: t_wtrdy}
        it = 0
        for ic in range(32):
            for g in range(4):
                sl = it % 2
                u8 = U8A if sl == 0 else U8B
                cv = CVT if sl == 0 else CVT2
                ta = P.dma("y", "su8", u8,
                           d_k1[ic * 128:(ic + 1) * 128, g * 1024:(g + 1) * 1024],
                           waits=[conv_hist.get(sl), lastu if it == 0 else None,
                                  t_wtrdy if it < 2 else None])
                tc_ = P.op("g", lambda e, u8=u8, cv=cv: e.tensor_scalar(
                    cv, u8, 4.0, -2.0, ALU.mult, ALU.add),
                    waits=[ta, sc_hist.get(sl)], inc="g")
                conv_hist[sl] = tc_
                srcp = cv.rearrange("p (h j) -> p h j", h=8)
                dst = s_w1x[g * 8:(g + 1) * 8, ic * 128:(ic + 1) * 128,
                            :].rearrange("h p j -> p h j")
                sc_hist[sl] = P.dma("y", "sc1" if sl == 0 else "sc1b",
                                    dst, srcp, waits=[tc_])
                it += 1
        t_w1rdy = [("sc1", P.cnt["sc1"]), ("sc1b", P.cnt["sc1b"])]
        for ic in range(32):
            sl = ic % 2
            u8 = U8A if sl == 0 else U8B
            cv = CVT if sl == 0 else CVT2
            ta = P.dma("y", "su8", u8, d_k2[ic * 128:(ic + 1) * 128, :],
                       waits=[conv_hist.get(sl)])
            tc_ = P.op("g", lambda e, u8=u8, cv=cv: e.tensor_scalar(
                cv, u8, 4.0, -2.0, ALU.mult, ALU.add),
                waits=[ta, sc_hist.get(sl)], inc="g")
            conv_hist[sl] = tc_
            sc_hist[sl] = P.dma("y", "sc2" if sl == 0 else "sc2b",
                                s_w2x[ic * 128:(ic + 1) * 128, :],
                                cv, waits=[tc_])
        t_w2rdy = [("sc2", P.cnt["sc2"]), ("sc2b", P.cnt["sc2b"])]


        stat_free = {}
        xst_free = {}
        fence = [None]
        TS_CUR = [TS]

        def ln_tile(ti, slot):
            st = o_stat + slot * 32
            stats = afs(st, 0, 128, 0, 12)
            mv = afs(st, 0, 128, 12, 14)
            vpe = afs(st, 0, 128, 14, 15)
            sd = afs(st, 0, 128, 15, 16)
            y0 = afs(st, 0, 128, 16, 17)
            t1 = afs(st, 0, 128, 17, 18)
            t2 = afs(st, 0, 128, 18, 19)
            t3 = afs(st, 0, 128, 19, 20)
            y1 = afs(st, 0, 128, 20, 21)
            nmr = afs(st, 0, 128, 21, 22)
            w = [x_tok[ti], stat_free.get(slot), fence[0]]
            P.op("v", lambda e: e.bn_stats(stats[:, 0:6], xt(ti)[:, 0:512]),
                 waits=w)
            tb = P.op("v", lambda e: e.bn_stats(stats[:, 6:12],
                                                xt(ti)[:, 512:1024]), inc="v")
            tg = P.op("v", lambda e: e.bn_aggr(mv, stats), waits=[tb], inc="v")
            tp = P.op("v", lambda e: e.tensor_scalar_add(vpe, mv[:, 1:2], EPS),
                      waits=[tg], inc="v")
            td = P.op("a", lambda e: e.activation(sd, vpe, AF.Sqrt), waits=[tp],
                      inc="a")
            t0 = P.op("v", lambda e: e.reciprocal(y0, sd), waits=[td], inc="v")
            ta_ = P.op("v", lambda e: e.tensor_mul(t1, y0, y0), waits=[t0],
                       inc="v")
            tb_ = P.op("v", lambda e: e.tensor_mul(t2, t1, vpe), waits=[ta_],
                       inc="v")
            tc_ = P.op("v", lambda e: e.tensor_scalar(t3, t2, -0.5, 1.5,
                                                      ALU.mult, ALU.add),
                       waits=[tb_], inc="v")
            td_ = P.op("v", lambda e: e.tensor_mul(y1, y0, t3), waits=[tc_],
                       inc="v")
            te_ = P.op("v", lambda e: e.scalar_tensor_tensor(
                nmr, mv[:, 0:1], -1.0, y1, ALU.mult, ALU.mult),
                waits=[td_], inc="v")
            tln = P.op("a", lambda e: e.activation(
                afs(o_xst, 0, 128, slot * C, (slot + 1) * C), xt(ti),
                AF.Identity, bias=nmr, scale=y1), waits=[te_, xst_free.get(slot)], inc="a")
            stat_free[slot] = tln
            return tln

        def ln_transpose(slot, o_dst, tokcol, tok_ln):
            tev = None
            for g in range(2):
                bk = 4 + g
                mm = None
                for u in range(4):
                    cc = g * 4 + u
                    mm = P.op("t", lambda e, slot=slot, cc=cc, bk=bk, u=u: e.transpose(
                        bank(bk, 0, 128, u * 128, (u + 1) * 128),
                        afs(o_xst, 0, 128, slot * C + cc * 128,
                            slot * C + (cc + 1) * 128), IDN),
                        waits=[tok_ln, t_idn, fence[0],
                               psum_free.get(bk) if u == 0 else None],
                        inc="t" if u == 3 else None)
                xst_free[slot] = mm
                base = o_dst + g * 4 * TS_CUR[0] + tokcol
                dst = ABT[:, base:base + 4 * TS_CUR[0]].rearrange(
                    "p (u c) -> p u c", c=TS_CUR[0])[:, :, 0:128]
                tev = P.op("a", lambda e, dst=dst, bk=bk: e.copy(
                    out=dst, in_=bank(bk)), waits=[mm], inc="a")
                psum_free[bk] = tev
            return tev

        def ffn_sublayer(entry=()):
            ab.push()
            o_xlnT = ab.alloc(8 * TS)
            o_h1 = ab.alloc(32 * TS)
            o_w0b = ab.alloc(2 * 1024)
            o_w1b = ab.alloc(2 * 4096)
            o_w2b = ab.alloc(2 * 1024)
            TS_CUR[0] = TS

            def h1_ap(i):
                return abs_(o_h1, 0, 128, i * TS, (i + 1) * TS)

            def h2_ap(i):
                return abs_(o_xlnT, 0, 128, i * TS, (i + 1) * TS)

            def xlnT_ap(ic, c0, c1):
                return abs_(o_xlnT, 0, 128, ic * TS + c0, ic * TS + c1)

            w0_free = {}
            w1_free = {}
            w2_free = {}
            prev_done = [None]
            mm_last = {}

            for ts in range(2):
                fence[0] = ("t", P.cnt["t"])
                t_xlnT = None
                for j in range(8):
                    ti = ts * 8 + j
                    tok_ln = ln_tile(ti, j % 2)
                    t_xlnT = ln_transpose(j % 2, o_xlnT, j * 128, tok_ln)

                h1_toks = {}
                for h1c in range(32):
                    wb = h1c % 2
                    wtok = P.dma("g", "w0a" if wb == 0 else "w0b",
                                 abs_(o_w0b, 0, 128, wb * 1024,
                                      (wb + 1) * 1024),
                                 s_w0d[h1c].rearrange("(a p) j -> p a j", p=128),
                                 waits=[t_w0rdy, w0_free.get(wb)] + list(entry))
                    rg = h1c % 2
                    mm = None
                    for half in range(2):
                        for icc in range(8):
                            mm = P.op("t", lambda e, half=half, icc=icc, wb=wb, rg=rg: e.matmul(
                                bank(rg * 2 + half),
                                abs_(o_w0b, 0, 128, wb * 1024 + icc * 128,
                                     wb * 1024 + (icc + 1) * 128),
                                xlnT_ap(icc, half * 512, (half + 1) * 512),
                                start=(icc == 0), stop=(icc == 7)),
                                waits=[wtok, t_xlnT,
                                       psum_free.get(rg * 2 + half) if icc == 0 else None],
                                inc="t" if (half == 1 and icc == 7) else None)
                    w0_free[wb] = mm
                    tev = P.op("a", lambda e, h1c=h1c, rg=rg: e.activation(
                        h1_ap(h1c), region(rg * 2), AF.Relu, scale=S0D),
                        waits=[mm, tv_s0,
                               prev_done[0] if h1c < 2 else None], inc="a")
                    psum_free[rg * 2] = tev
                    psum_free[rg * 2 + 1] = tev
                    h1_toks[h1c] = tev

                t_m1mm = ("t", P.cnt["t"])
                for Q in range(4):
                    h2_toks = {}
                    for idx in range(8):
                        h2c = Q * 8 + idx
                        wb = idx % 2
                        wtok = P.dma("a" if wb == 0 else "g",
                                     "w1a" if wb == 0 else "w1b",
                                     abs_(o_w1b, 0, 128, wb * 4096,
                                          (wb + 1) * 4096),
                                     s_w1x[h2c].rearrange("(a p) j -> p a j",
                                                          p=128),
                                     waits=t_w1rdy + [w1_free.get(wb)] + list(entry))
                        rg = idx % 2
                        mm = None
                        for half in range(2):
                            for h1c in range(32):
                                mm = P.op("t", lambda e, half=half, h1c=h1c, wb=wb, rg=rg: e.matmul(
                                    bank(rg * 2 + half),
                                    abs_(o_w1b, 0, 128, wb * 4096 + h1c * 128,
                                         wb * 4096 + (h1c + 1) * 128),
                                    h1_ap(h1c)[:, half * 512:(half + 1) * 512],
                                    start=(h1c == 0), stop=(h1c == 31)),
                                    waits=[wtok, h1_toks[31],
                                           psum_free.get(rg * 2 + half) if h1c == 0 else None],
                                    inc="t" if (half == 1 and h1c == 31) else None)
                        w1_free[wb] = mm
                        tev = P.op("a", lambda e, idx=idx, rg=rg: e.activation(
                            h2_ap(idx), region(rg * 2), AF.Relu, scale=S1R),
                            waits=[mm, tv_s1, t_m1mm], inc="a")
                        psum_free[rg * 2] = tev
                        psum_free[rg * 2 + 1] = tev
                        h2_toks[idx] = tev

                    for tcg in range(2):
                        mm_last = {}
                        for idx in range(8):
                            h2c = Q * 8 + idx
                            wb = idx % 2
                            wtok = P.dma("a", "w2a" if wb == 0 else "w2b",
                                         abs_(o_w2b, 0, 128, wb * 1024,
                                              (wb + 1) * 1024),
                                         s_w2x[h2c * 128:(h2c + 1) * 128, :],
                                         waits=t_w2rdy + [w2_free.get(wb)] + list(entry))
                            for u in range(4):
                                tc = tcg * 4 + u
                                for half in range(2):
                                    bk = u * 2 + half
                                    mm_last[(u, half)] = P.op(
                                        "t", lambda e, idx=idx, tc=tc, half=half, wb=wb, bk=bk: e.matmul(
                                            bank(bk),
                                            h2_ap(idx)[:, tc * 128:(tc + 1) * 128],
                                            abs_(o_w2b, 0, 128,
                                                 wb * 1024 + half * 512,
                                                 wb * 1024 + (half + 1) * 512),
                                            start=(idx == 0), stop=(idx == 7)),
                                        waits=[wtok, h2_toks[idx],
                                               psum_free.get(bk) if idx == 0 else None],
                                        inc="t" if (u == 3 and half == 1) or idx == 7 else None)
                            w2_free[wb] = mm_last[(3, 1)]
                        for u in range(4):
                            tc = tcg * 4 + u
                            ti = ts * 8 + tc
                            for half in range(2):
                                bk = u * 2 + half
                                tev = P.op("v", lambda e, ti=ti, half=half, bk=bk: e.scalar_tensor_tensor(
                                    xt(ti)[:, half * 512:(half + 1) * 512],
                                    bank(bk), S2H,
                                    xt(ti)[:, half * 512:(half + 1) * 512],
                                    ALU.mult, ALU.add),
                                    waits=[mm_last[(u, half)], tv_s2,
                                           x_tok[ti]],
                                    inc="v")
                                psum_free[bk] = tev
                                x_tok[ti] = tev
                prev_done[0] = mm_last[(3, 1)]
            ab.pop()

        def emit_qkv(entry=()):
            ab.push()
            o_xl2 = ab.alloc(8 * T)
            o_wq4 = [ab.alloc(8 * 1024) for _ in range(3)]
            o_stg = ab.alloc(2 * T)
            TS_CUR[0] = T
            fence[0] = ("t", P.cnt["t"])

            def xl2_ap(ic, c0, c1):
                return abs_(o_xl2, 0, 128, ic * T + c0, ic * T + c1)

            t_xl2 = None
            for ti in range(16):
                tok_ln = ln_tile(ti, ti % 2)
                t_xl2 = ln_transpose(ti % 2, o_xl2, ti * 128, tok_ln)

            wt_toks = []
            for wi in range(3):
                for ic in range(8):
                    wt_toks.append(P.dma(
                        "g", "sqw",
                        abs_(o_wq4[wi], 0, 128, ic * 1024, (ic + 1) * 1024),
                        s_wt[wi, ic * 128:(ic + 1) * 128, :],
                        waits=[t_wtrdy] + list(entry)))
            t_wload = ("sqw", P.cnt["sqw"])

            stg_free = {}
            for which in range(2):
                dst_dram = s_q if which == 0 else s_kgi
                bias = afs(o_bq if which == 0 else o_bk, 0, 128, 0, 8)
                scale = 0.125 if which == 0 else 1.0
                bias_tok = tv_bq if which == 0 else t_bk
                for oc in range(8):
                    slot = oc % 2
                    evs = []
                    for nt in range(4):
                        bk = (oc * 4 + nt) % 4
                        mm = None
                        for icc in range(8):
                            mm = P.op("t", lambda e, which=which, oc=oc, nt=nt, icc=icc, bk=bk: e.matmul(
                                bank(bk),
                                abs_(o_wq4[which], 0, 128,
                                     icc * 1024 + oc * 128,
                                     icc * 1024 + (oc + 1) * 128),
                                xl2_ap(icc, nt * 512, (nt + 1) * 512),
                                start=(icc == 0), stop=(icc == 7)),
                                waits=[t_wload, t_xl2,
                                       psum_free.get(bk) if icc == 0 else None],
                                inc="t" if icc == 7 else None)
                        tev = P.op("a", lambda e, slot=slot, nt=nt, bk=bk, bias=bias, oc=oc, scale=scale: e.activation(
                            abs_(o_stg, 0, 128, slot * T + nt * 512,
                                 slot * T + (nt + 1) * 512),
                            bank(bk), AF.Identity, bias=bias[:, oc:oc + 1],
                            scale=scale),
                            waits=[mm, bias_tok, stg_free.get(slot)], inc="a")
                        psum_free[bk] = tev
                        evs.append(tev)
                    stg_free[slot] = P.dma(
                        "y", "sga" if slot == 0 else "sgb",
                        dst_dram[oc * 128:(oc + 1) * 128, :],
                        abs_(o_stg, 0, 128, slot * T, (slot + 1) * T),
                        waits=evs)
            for tc in range(16):
                slot = tc % 2
                rg = tc % 2
                mm = None
                for icc in range(8):
                    for half in range(2):
                        mm = P.op("t", lambda e, tc=tc, icc=icc, half=half, rg=rg: e.matmul(
                            bank(rg * 2 + half),
                            xl2_ap(icc, tc * 128, (tc + 1) * 128),
                            abs_(o_wq4[2], 0, 128, icc * 1024 + half * 512,
                                 icc * 1024 + (half + 1) * 512),
                            start=(icc == 0), stop=(icc == 7)),
                            waits=[t_wload, t_xl2,
                                   psum_free.get(rg * 2 + half) if icc == 0 else None],
                            inc="t" if (icc == 7 and half == 1) else None)
                tev = P.op("v", lambda e, slot=slot, rg=rg: e.tensor_tensor(
                    abs_(o_stg, 0, 128, slot * T, slot * T + 1024),
                    region(rg * 2), BVB, ALU.add),
                    waits=[mm, t_bias, stg_free.get(slot)], inc="v")
                psum_free[rg * 2] = tev
                psum_free[rg * 2 + 1] = tev
                stg_free[slot] = P.dma(
                    "y", "sga" if slot == 0 else "sgb",
                    s_vgi[tc * 128:(tc + 1) * 128, :],
                    abs_(o_stg, 0, 128, slot * T, slot * T + 1024), waits=[tev])

            t_ydall = [("sga", P.cnt["sga"]), ("sgb", P.cnt["sgb"])]
            rg_pairs = [[0, 1], [2, 3], [4, 5], [6, 7]]
            tcc_k = P.op("g", lambda e: e.collective_compute(
                "AllGather", ALU.bypass, replica_groups=rg_pairs,
                ins=[s_kgi.ap().opt()], outs=[s_kgo.ap().opt()]),
                waits=t_ydall, inc="cc")
            tcc_v = P.op("g", lambda e: e.collective_compute(
                "AllGather", ALU.bypass, replica_groups=rg_pairs,
                ins=[s_vgi.ap().opt()], outs=[s_vgo.ap().opt()]),
                waits=[], inc="cc")
            if dbg:
                P.dma("g", "gdb", d_dbgq[:, :], s_q[:, :],
                      waits=t_ydall + [tcc_k, tcc_v])
                P.dma("g", "gdb", d_dbgk[:, :], s_kgo[:, :],
                      waits=[tcc_k, tcc_v])
                P.dma("g", "gdb", d_dbgv[:, :], s_vgo[:, :],
                      waits=[tcc_k, tcc_v])
            ab.pop()
            return (tcc_k, tcc_v)

        ngm_free = {}
        pt_free = {}
        tmb_free = {}
        rcp_free = {}

        def emit_attention(tcc):
            ab.push()
            o_set = []
            for s in range(2):
                o_set.append(dict(qa=ab.alloc(T), qb=ab.alloc(T),
                                  ka=ab.alloc(2 * T), kb=ab.alloc(2 * T),
                                  vp=ab.alloc(64 * 65)))
            o_pt = ab.alloc(2 * 1024)
            o_afm = ab.alloc(2 * 2 * T)
            o_wpm = ab.alloc(8 * 1024)
            o_tmb = ab.alloc(512)
            o_ngm = ab.alloc(128)
            fence[0] = ("t", P.cnt["t"])

            for pc in range(8):
                P.dma("g", "swp",
                      abs_(o_wpm, 0, 128, pc * 1024, (pc + 1) * 1024),
                      s_wt[3, pc * 128:(pc + 1) * 128, :],
                      waits=[t_wtrdy, ("cc", 2)])
            t_wpl = ("swp", 128)
            lq1 = afs(o_lam, 0, 8, 0, 64)
            lk1 = afs(o_lam, 0, 8, 64, 128)
            lq2 = afs(o_lam, 0, 8, 128, 192)
            lk2 = afs(o_lam, 0, 8, 192, 256)
            l1c = afs(o_lam, 0, 8, 256, 257)
            l2c = afs(o_lam, 0, 8, 257, 258)
            ddc = afs(o_lam, 0, 8, 258, 259)
            lvs = afs(o_lam, 0, 1, 259, 260)
            tk = P.op("v", lambda e: e.tensor_mul(lq1, lq1, lk1), waits=[t_lam],
                      inc="v")
            tk = P.op("v", lambda e: e.tensor_reduce(l1c, lq1, AX.X, ALU.add),
                      waits=[tk], inc="v")
            tk = P.op("v", lambda e: e.tensor_scalar_min(l1c, l1c, 5.0),
                      waits=[tk], inc="v")
            tk1 = P.op("a", lambda e: e.activation(l1c, l1c, AF.Exp),
                       waits=[tk], inc="a")
            tk = P.op("v", lambda e: e.tensor_mul(lq2, lq2, lk2), waits=[t_lam],
                      inc="v")
            tk = P.op("v", lambda e: e.tensor_reduce(l2c, lq2, AX.X, ALU.add),
                      waits=[tk], inc="v")
            tk = P.op("v", lambda e: e.tensor_scalar_min(l2c, l2c, 5.0),
                      waits=[tk], inc="v")
            tk2 = P.op("a", lambda e: e.activation(l2c, l2c, AF.Exp),
                       waits=[tk], inc="a")
            tk = P.op("v", lambda e: e.tensor_sub(ddc, l1c, l2c),
                      waits=[tk1, tk2], inc="v")
            tmm = P.op("t", lambda e: e.matmul(bank(6, 0, 1, 0, 1), ddc, ONES8,
                                               start=True, stop=True),
                       waits=[tk, tv_one8, psum_free.get(6)], inc="t")
            tlv = P.op("a", lambda e: e.activation(lvs, bank(6, 0, 1, 0, 1),
                                                   AF.Copy, bias=-0.8,
                                                   scale=-0.125),
                       waits=[tmm], inc="a")
            tmm = P.op("t", lambda e: e.matmul(bank(6, 0, 128, 0, 1), ONESF,
                                               lvs, start=True, stop=True),
                       waits=[tlv, tv_onef], inc="t")
            tlvn = P.op("v", lambda e: e.tensor_copy(
                out=LVN, in_=bank(6, 0, 128, 0, 1)), waits=[tmm], inc="v")
            psum_free[6] = tlvn
            t_wpm = None
            for pc in range(4):
                t_wpm = P.op("v", lambda e, pc=pc: e.scalar_tensor_tensor(
                    abs_(o_wpm, 0, 128, (4 + pc) * 1024, (5 + pc) * 1024),
                    abs_(o_wpm, 0, 128, pc * 1024, (pc + 1) * 1024), LVN,
                    abs_(o_wpm, 0, 128, (4 + pc) * 1024, (5 + pc) * 1024),
                    ALU.mult, ALU.add),
                    waits=[tlvn, t_wpl], inc="v")

            set_free = {}
            afm_free = {}

            def stage1(p):
                s = p % 2
                oS = o_set[s]
                bfree = set_free.get(s)
                asem = "aa0" if s == 0 else "aa1"
                tqa = P.dma("g", asem, abs_(oS["qa"], 0, 64, 0, T),
                            s_q[128 * p:128 * p + 64, :],
                            waits=[bfree, ("cc", 2)])
                P.dma("g", asem, abs_(oS["qb"], 0, 64, 0, T),
                      s_q[128 * p + 64:128 * p + 128, :], waits=[bfree])
                P.dma("g", asem, abs_(oS["ka"], 0, 64, 0, T),
                      s_kgo[128 * p:128 * p + 64, :], waits=[bfree])
                P.dma("g", asem, abs_(oS["ka"], 0, 64, T, 2 * T),
                      s_kgo[C + 128 * p:C + 128 * p + 64, :], waits=[bfree])
                P.dma("g", asem, abs_(oS["kb"], 0, 64, 0, T),
                      s_kgo[128 * p + 64:128 * p + 128, :], waits=[bfree])
                P.dma("g", asem, abs_(oS["kb"], 0, 64, T, 2 * T),
                      s_kgo[C + 128 * p + 64:C + 128 * p + 128, :],
                      waits=[bfree])
                tone_a = P.op("v", lambda e, oS=oS: e.memset(
                    abs_(oS["ka"], 64, 65, 0, 2 * T), 1.0), waits=[bfree],
                    inc="v")
                tone_b = P.op("v", lambda e, oS=oS: e.memset(
                    abs_(oS["kb"], 64, 65, 0, 2 * T), 1.0), waits=[bfree],
                    inc="v")
                for hh in range(2):
                    cbase = 128 * p + hh * 64
                    for g in range(4):
                        dst = ABT[0:128,
                                  oS["vp"] + hh * 2080 + g * 8 * 65:
                                  oS["vp"] + hh * 2080 + (g + 1) * 8 * 65
                                  ].rearrange("p (j d) -> p j d",
                                              d=65)[:, :, 0:64]
                        srcv = s_vgo[g * 8 * 128:(g + 1) * 8 * 128,
                                     cbase:cbase + 64].rearrange(
                            "(j p) d -> p j d", p=128)
                        P.dma("y", "av0" if s == 0 else "av1", dst, srcv,
                              waits=[bfree, ("cc", 2)])
                tone_v = P.op("v", lambda e, oS=oS: e.memset(
                    ABT[0:128, oS["vp"]:oS["vp"] + 64 * 65].rearrange(
                        "p (j d) -> p j d", d=65)[:, :, 64:65], 1.0),
                    waits=[bfree], inc="v")
                t_load = (asem, P.cnt[asem])
                t_vload = ("av0" if s == 0 else "av1",
                           P.cnt["av0" if s == 0 else "av1"])

                def qa(c0, c1, oS=oS):
                    return abs_(oS["qa"], 0, 64, c0, c1)

                def qb(c0, c1, oS=oS):
                    return abs_(oS["qb"], 0, 64, c0, c1)

                def ka(c0, c1, oS=oS):
                    return abs_(oS["ka"], 0, 64, c0, c1)

                def kb(c0, c1, oS=oS):
                    return abs_(oS["kb"], 0, 64, c0, c1)

                mbuf = afs(o_mbuf, 0, 128, 0, 32)
                m8 = afs(o_m8, 0, 128, 0, 8)
                mm_tok = None
                tr = None
                for hi, (qh, kh) in enumerate([(qa, ka), (qb, kb)]):
                    for qt in range(16):
                        for kc in range(8):
                            bk = kc % 2
                            mma = P.op("t", lambda e, qh=qh, kh=kh, qt=qt, kc=kc, bk=bk: e.matmul(
                                bank(bk), qh(qt * 128, (qt + 1) * 128),
                                kh(kc * 512, (kc + 1) * 512),
                                start=True, stop=True),
                                waits=[t_load, psum_free.get(bk)], inc="t")
                            tr = P.op("v", lambda e, kc=kc, bk=bk, m8=m8: e.tensor_reduce(
                                m8[:, kc:kc + 1], bank(bk), AX.X, ALU.max),
                                waits=[mma,
                                       ngm_free.get(0) if (hi == 0 and qt == 0 and kc == 0) else None,
                                       mm_tok if kc == 0 else None],
                                inc="v")
                            psum_free[bk] = tr
                        mm_tok = P.op("v", lambda e, hi=hi, qt=qt, mbuf=mbuf, m8=m8: e.tensor_reduce(
                            mbuf[:, hi * 16 + qt:hi * 16 + qt + 1], m8, AX.X,
                            ALU.max), waits=[tr], inc="v")
                tmt = P.op("t", lambda e, mbuf=mbuf: e.transpose(
                    bank(0, 0, 32, 0, 128), mbuf, IDN),
                    waits=[mm_tok, psum_free.get(0), t_idn], inc="t")
                tng = P.op("a", lambda e: e.activation(
                    abs_(o_ngm, 0, 32, 0, 128), bank(0, 0, 32, 0, 128),
                    AF.Copy, scale=-1.0),
                    waits=[tmt, ngm_free.get(0)], inc="a")
                psum_free[0] = tng
                def emit_mrow(s=s, oS=oS, tng=tng, bfree=bfree):
                    gmsem = "gm0" if s == 0 else "gm1"
                    P.dma("g", gmsem, ABT[64:65, oS["qa"]:oS["qa"] + T],
                          abs_(o_ngm, 0, 16, 0, 128), waits=[tng, bfree])
                    P.dma("g", gmsem, ABT[64:65, oS["qb"]:oS["qb"] + T],
                          abs_(o_ngm, 16, 32, 0, 128), waits=[tng, bfree])
                    t_mrow = (gmsem, P.cnt[gmsem])
                    ngm_free[0] = t_mrow
                    return t_mrow

                def kaf(c0, c1, oS=oS):
                    return abs_(oS["ka"], 0, 65, c0, c1)

                def kbf(c0, c1, oS=oS):
                    return abs_(oS["kb"], 0, 65, c0, c1)

                def qaf(c0, c1, oS=oS):
                    return abs_(oS["qa"], 0, 65, c0, c1)

                def qbf(c0, c1, oS=oS):
                    return abs_(oS["qb"], 0, 65, c0, c1)

                return dict(p=p, s=s, oS=oS, emit_mrow=emit_mrow, tone_a=tone_a,
                            tone_b=tone_b, tone_v=tone_v, t_vload=t_vload,
                            kaf=kaf, kbf=kbf, qaf=qaf, qbf=qbf)

            def stage2(ctx):
                p = ctx["p"]
                t_mrow = ctx["t_mrow"]
                oS = ctx["oS"]
                grp = p // 2
                gslot = grp % 2
                o_af = o_afm + gslot * 2 * T + (p % 2) * T
                pv_last = {}
                for qg in range(4):
                    for hi in range(2):
                        kf_ = ctx["kaf"] if hi == 0 else ctx["kbf"]
                        qf_ = ctx["qaf"] if hi == 0 else ctx["qbf"]
                        vbase = oS["vp"] + hi * 2080
                        pvbank = 6 + hi
                        pvl = None
                        for kreg in range(16):
                            rb = 2 + (kreg % 2) * 2
                            mm = None
                            for jj in range(2):
                                j = kreg * 2 + jj
                                mm = P.op("t", lambda e, j=j, jj=jj, rb=rb, qg=qg, kf_=kf_, qf_=qf_: e.matmul(
                                    bank(rb + jj),
                                    kf_(j * 128, (j + 1) * 128),
                                    qf_(qg * 512, (qg + 1) * 512),
                                    start=True, stop=True),
                                    waits=[t_mrow, ctx["tone_a"],
                                           ctx["tone_b"],
                                           psum_free.get(rb + jj)],
                                    inc="t" if jj == 1 else None)
                            pr = kreg % 2
                            tex = P.op("a", lambda e, pr=pr, rb=rb: e.activation(
                                abs_(o_pt, 0, 128, pr * 1024, (pr + 1) * 1024),
                                region(rb), AF.Exp),
                                waits=[mm, pt_free.get(pr)], inc="a")
                            psum_free[rb] = tex
                            psum_free[rb + 1] = tex
                            for jj in range(2):
                                j = kreg * 2 + jj
                                pvl = P.op("t", lambda e, j=j, jj=jj, pr=pr, vbase=vbase, pvbank=pvbank, kreg=kreg: e.matmul(
                                    bank(pvbank, 0, 65),
                                    ABT[0:128, vbase + j * 65:
                                        vbase + (j + 1) * 65],
                                    abs_(o_pt, 0, 128, pr * 1024 + jj * 512,
                                         pr * 1024 + (jj + 1) * 512),
                                    start=(kreg == 0 and jj == 0),
                                    stop=(kreg == 15 and jj == 1)),
                                    waits=[tex, ctx["tone_v"], ctx["t_vload"]] +
                                          ([psum_free.get(pvbank)] if (kreg == 0 and jj == 0) else []),
                                    inc="t" if jj == 1 else None)
                                pt_free[pr] = pvl
                        pv_last[hi] = pvl
                    trA = P.op("v", lambda e: e.reciprocal(
                        afs(o_rcp, 64, 65, 0, 512), bank(6, 64, 65)),
                        waits=[pv_last[0], rcp_free.get(0)], inc="v")
                    trB = P.op("v", lambda e: e.reciprocal(
                        afs(o_rcp, 64, 65, 512, 1024), bank(7, 64, 65)),
                        waits=[pv_last[1]], inc="v")
                    tshp = P.dma("g", "grc", afs(o_rcp, 0, 1, 0, 1024),
                                 afs(o_rcp, 64, 65, 0, 1024),
                                 waits=[trA, trB])
                    tcA = P.op("v", lambda e: e.tensor_copy(
                        out=RB16A, in_=afs(o_rcp, 0, 1, 0, 512)),
                        waits=[tshp], inc="v")
                    tcB = P.op("v", lambda e: e.tensor_copy(
                        out=RB16B, in_=afs(o_rcp, 0, 1, 512, 1024)),
                        waits=[tshp], inc="v")
                    rcp_free[0] = tcB
                    tm1 = P.op("t", lambda e: e.matmul(
                        bank(2, 0, 64), ONESB, RB16A, start=True, stop=True),
                        waits=[tcA, tv_oneb, psum_free.get(2)], inc="t")
                    tm2 = P.op("t", lambda e: e.matmul(
                        bank(3, 0, 64), ONESB, RB16B, start=True, stop=True),
                        waits=[tcB, psum_free.get(3)], inc="t")
                    trb = P.op("a", lambda e: e.copy(
                        out=afs(o_rbc, 0, 64, 0, 1024),
                        in_=region(2, 1024, 0, 64)),
                        waits=[tm1, tm2], inc="a")
                    psum_free[2] = trb
                    psum_free[3] = trb
                    evA = P.op("v", lambda e, o_af=o_af, qg=qg: e.tensor_tensor(
                        abs_(o_af, 0, 64, qg * 512, (qg + 1) * 512),
                        bank(6, 0, 64), afs(o_rbc, 0, 64, 0, 512), ALU.mult),
                        waits=[trb,
                               afm_free.get(gslot) if qg == 0 else None],
                        inc="v")
                    evB = P.op("v", lambda e: e.tensor_tensor(
                        abs_(o_tmb, 0, 64, 0, 512),
                        bank(7, 0, 64), afs(o_rbc, 0, 64, 512, 1024),
                        ALU.mult),
                        waits=[trb, tmb_free.get(0)], inc="v")
                    psum_free[6] = evA
                    psum_free[7] = evB
                    tsh = P.dma("g", "gt0" if gslot == 0 else "gt1",
                                abs_(o_af, 64, 128, qg * 512, (qg + 1) * 512),
                                abs_(o_tmb, 0, 64, 0, 512), waits=[evB])
                    tmb_free[0] = tsh
                set_free[ctx["s"]] = pvl

            def proj_group(grp):
                gslot = grp % 2
                gtsem = "gt0" if gslot == 0 else "gt1"
                t_grp = [("v", P.cnt["v"]), (gtsem, P.cnt[gtsem])]
                proj_last = None
                for tc in range(16):
                    rg = tc % 2
                    mm = None
                    for pc2 in range(2):
                        pcw = grp * 2 + pc2
                        for half in range(2):
                            mm = P.op("t", lambda e, tc=tc, pc2=pc2, pcw=pcw, half=half, rg=rg, gslot=gslot: e.matmul(
                                bank(rg * 2, 0, 128, half * 512,
                                     (half + 1) * 512),
                                abs_(o_afm + gslot * 2 * T + pc2 * T,
                                     0, 128, tc * 128, (tc + 1) * 128),
                                abs_(o_wpm, 0, 128,
                                     pcw * 1024 + half * 512,
                                     pcw * 1024 + (half + 1) * 512),
                                start=(pc2 == 0), stop=(pc2 == 1)),
                                waits=t_grp + [t_wpm,
                                               (psum_free.get(rg * 2) if half == 0 else psum_free.get(rg * 2 + 1)) if pc2 == 0 else None],
                                inc="t" if (pc2 == 1 and half == 1) else None)
                    tv1 = P.op("v", lambda e, tc=tc, rg=rg: e.tensor_tensor(
                        xt(tc), region(rg * 2), xt(tc), ALU.add),
                        waits=[mm, x_tok[tc]], inc="v")
                    psum_free[rg * 2] = tv1
                    psum_free[rg * 2 + 1] = tv1
                    x_tok[tc] = tv1
                    proj_last = tv1
                afm_free[gslot] = proj_last

            prev = None
            for p in range(8):
                ctx = stage1(p)
                if prev is not None:
                    stage2(prev)
                    if prev["p"] % 2 == 1:
                        proj_group(prev["p"] // 2)
                ctx["t_mrow"] = ctx["emit_mrow"]()
                prev = ctx
            stage2(prev)
            proj_group(3)
            for tc in range(16):
                tvb = P.op("v", lambda e, tc=tc: e.tensor_tensor(
                    xt(tc), xt(tc), BPB, ALU.add),
                    waits=[x_tok[tc], t_bias], inc="v")
                x_tok[tc] = tvb
            ab.pop()

        ffn_sublayer()
        if stage in ("qkv", "x2", "full"):
            tcc = emit_qkv(entry=[("t", P.cnt["t"])])
        if stage in ("x2", "full"):
            emit_attention(tcc)
        if stage == "full":
            t_att = [("t", P.cnt["t"]), ("v", P.cnt["v"]), ("a", P.cnt["a"]),
                     ("gt0", P.cnt["gt0"]), ("gt1", P.cnt["gt1"]),
                     ("grc", P.cnt["grc"]),
                     ("gm0", P.cnt["gm0"]), ("gm1", P.cnt["gm1"])]
            ffn_sublayer(entry=t_att)

        for i in range(NTILE):
            P.dma("g", "go", d_out[i * 128:(i + 1) * 128, :], xt(i),
                  waits=[x_tok[i]])
        final_waits = [("go", P.cnt["go"])]
        if dbg:
            final_waits.append(("gdb", P.cnt["gdb"]))
        P.op("g", lambda e: e.nop(), waits=final_waits)

        print("[build] ops: " + " ".join(f"{e}:{len(P.q[e])}" for e in ENGS) +
              f"  AF hwm {af.hwm}/{NF}  AB hwm {ab.hwm}/{NB}")

        with nc.Block() as block:
            @block.tensor
            def _(eng):
                for f in P.q["t"]:
                    f(eng)

            @block.vector
            def _(eng):
                for f in P.q["v"]:
                    f(eng)

            @block.scalar
            def _(eng):
                for f in P.q["a"]:
                    f(eng)

            @block.gpsimd
            def _(eng):
                for f in P.q["g"]:
                    f(eng)

            @block.sync
            def _(eng):
                for f in P.q["y"]:
                    f(eng)

    return nc


def host_inputs(inputs, core):
    b, h = core // 2, core % 2
    f32 = np.float32
    x = np.ascontiguousarray(
        np.asarray(inputs["L0_attn_x"], dtype=f32)[b, h * T:(h + 1) * T])
    return {
        "x_in": x,
        "wqt": np.ascontiguousarray(np.asarray(inputs["L0_attn_Wq"], f32).T),
        "wkt": np.ascontiguousarray(np.asarray(inputs["L0_attn_Wk"], f32).T),
        "wvt": np.ascontiguousarray(np.asarray(inputs["L0_attn_Wv"], f32).T),
        "wpt": np.ascontiguousarray(np.asarray(inputs["L0_attn_Wp"], f32).T),
        "bq_c": np.ascontiguousarray(
            np.asarray(inputs["L0_attn_bq"], f32).reshape(8, 128).T),
        "bk_c": np.ascontiguousarray(
            np.asarray(inputs["L0_attn_bk"], f32).reshape(8, 128).T),
        "b_row": np.ascontiguousarray(np.stack(
            [np.asarray(inputs["L0_attn_bv"], f32),
             np.asarray(inputs["L0_attn_bp"], f32)])),
        "lam": np.ascontiguousarray(np.concatenate(
            [np.asarray(inputs["L0_attn_lambda_q1"], f32).reshape(8, 64),
             np.asarray(inputs["L0_attn_lambda_k1"], f32).reshape(8, 64),
             np.asarray(inputs["L0_attn_lambda_q2"], f32).reshape(8, 64),
             np.asarray(inputs["L0_attn_lambda_k2"], f32).reshape(8, 64)],
            axis=1)),
        "k0": np.ascontiguousarray(np.asarray(inputs["L0_ffn_k0"]).view(np.uint8)),
        "k1": np.ascontiguousarray(np.asarray(inputs["L0_ffn_k1"]).view(np.uint8)),
        "k2": np.ascontiguousarray(np.asarray(inputs["L0_ffn_k2"]).view(np.uint8)),
        "s_r": np.ascontiguousarray(np.broadcast_to(np.stack(
            [np.asarray(inputs["L0_ffn_s0"], f32).reshape(()),
             np.asarray(inputs["L0_ffn_s1"], f32).reshape(()),
             np.asarray(inputs["L0_ffn_s2"], f32).reshape(())])[None, :],
            (128, 3))),
        "idn": np.eye(128, dtype=f32),
    }


def kernel(**inputs):
    nc = build_program("full")
    in_maps = [host_inputs(inputs, c) for c in range(8)]
    res = run_bass_kernel_spmd(nc, in_maps, core_ids=list(range(8)))
    out = np.empty((4, 4096, 1024), dtype=np.float32)
    for c in range(8):
        b, h = c // 2, c % 2
        out[b, h * T:(h + 1) * T] = res.results[c]["out"]
    return out


# version 06 (final)
# speedup vs baseline: 1.2286x; 1.2286x over previous
from contextlib import ExitStack

import numpy as np
import ml_dtypes

import concourse.bass as bass
import concourse.mybir as mybir
from concourse.bass_utils import run_bass_kernel_spmd

F32 = mybir.dt.float32
BF16 = mybir.dt.bfloat16
U8 = mybir.dt.uint8
AF = mybir.ActivationFunctionType
ALU = mybir.AluOpType
AX = mybir.AxisListType
bfdt = ml_dtypes.bfloat16

T = 2048
C = 1024
H1 = 4096
TS = 1024
NTILE = 16
EPS = 1e-5

ENGS = "tvagy"


class Prog:
    def __init__(self, nc):
        self.nc = nc
        self.q = {e: [] for e in ENGS}
        self.sems = {}
        self.cnt = {}
        self.waited = {e: {} for e in ENGS}

    def mksem(self, ctx, *names):
        for n in names:
            self.sems[n] = ctx.enter_context(self.nc.semaphore(n))
            self.cnt[n] = 0

    def op(self, e, f, waits=(), inc=None, inc_n=1):
        tok = None
        if inc is not None:
            self.cnt[inc] += inc_n
            tok = (inc, self.cnt[inc])
        merged = {}
        for w in waits:
            if w is None:
                continue
            s, v = w
            if v > merged.get(s, -1):
                merged[s] = v
        wl = []
        for s, v in merged.items():
            if self.waited[e].get(s, -1) >= v:
                continue
            self.waited[e][s] = v
            wl.append((s, v))
        sems = self.sems

        def run(eng, wl=wl, f=f, inc=inc, inc_n=inc_n):
            for s, v in wl:
                eng.wait_ge(sems[s], v)
            inst = f(eng)
            if inc is not None:
                inst.then_inc(sems[inc], inc_n)
        self.q[e].append(run)
        return tok

    def dma(self, e, sem, out, in_, waits=()):
        return self.op(e, lambda eng: eng.dma_start(out=out, in_=in_),
                       waits=waits, inc=sem, inc_n=16)


class Arena:
    def __init__(self, t, ncols, name):
        self.t = t
        self.n = ncols
        self.off = 0
        self.stack = []
        self.hwm = 0
        self.name = name

    def alloc(self, cols, align=8):
        self.off = (self.off + align - 1) // align * align
        o = self.off
        self.off += cols
        self.hwm = max(self.hwm, self.off)
        assert self.off <= self.n, f"{self.name} overflow {self.off}>{self.n}"
        return o

    def push(self):
        self.stack.append(self.off)

    def pop(self):
        self.off = self.stack.pop()


def build_program(stage="full", dbg=False):
    nc = bass.Bass("TRN2", target_bir_lowering=False)

    d_x = nc.dram_tensor("x_in", [T, C], F32, kind="ExternalInput")
    d_w = {n: nc.dram_tensor(n, [C, C], F32, kind="ExternalInput")
           for n in ("wqt", "wkt", "wvt", "wpt")}
    d_bqc = nc.dram_tensor("bq_c", [128, 8], F32, kind="ExternalInput")
    d_bkc = nc.dram_tensor("bk_c", [128, 8], F32, kind="ExternalInput")
    d_brow = nc.dram_tensor("b_row", [2, C], F32, kind="ExternalInput")
    d_lam = nc.dram_tensor("lam", [8, 256], F32, kind="ExternalInput")
    d_k0 = nc.dram_tensor("k0", [2 * C, H1], U8, kind="ExternalInput")
    d_k1 = nc.dram_tensor("k1", [H1, H1], U8, kind="ExternalInput")
    d_k2 = nc.dram_tensor("k2", [H1, C], U8, kind="ExternalInput")
    d_s = nc.dram_tensor("s_r", [128, 3], F32, kind="ExternalInput")
    d_idn = nc.dram_tensor("idn", [128, 128], F32, kind="ExternalInput")
    d_out = nc.dram_tensor("out", [T, C], F32, kind="ExternalOutput")

    s_w0d = nc.dram_tensor("s_w0d", [32, C, 128], BF16)
    s_w1x = nc.dram_tensor("s_w1x", [32, H1, 128], BF16)
    s_w2x = nc.dram_tensor("s_w2x", [H1, C], BF16)
    s_wt = nc.dram_tensor("s_wt", [4, C, C], BF16)
    s_q = nc.dram_tensor("s_q", [C, T], BF16)
    s_kgi = nc.dram_tensor("s_kgi", [C, T], BF16)
    s_kgo = nc.dram_tensor("s_kgo", [2 * C, T], BF16)
    s_vgi = nc.dram_tensor("s_vgi", [T, C], BF16)
    s_vgo = nc.dram_tensor("s_vgo", [2 * T, C], BF16)

    if dbg:
        d_dbgq = nc.dram_tensor("dbg_q", [C, T], BF16, kind="ExternalOutput")
        d_dbgk = nc.dram_tensor("dbg_kg", [2 * C, T], BF16, kind="ExternalOutput")
        d_dbgv = nc.dram_tensor("dbg_vg", [2 * T, C], BF16, kind="ExternalOutput")
        d_dbga = nc.dram_tensor("dbg_afm", [128, 2 * T], BF16, kind="ExternalOutput")

    with ExitStack() as ctx:
        NF, NB, NU = 23296, 58600, 2048
        AFT = ctx.enter_context(nc.sbuf_tensor("AF", [128, NF], F32))
        ABT = ctx.enter_context(nc.sbuf_tensor("AB", [128, NB], BF16))
        AUT = ctx.enter_context(nc.sbuf_tensor("AU", [128, NU], U8))
        PSF = ctx.enter_context(nc.psum_tensor("PSF", [128, 4096], F32))
        af = Arena(AFT, NF, "AF")
        ab = Arena(ABT, NB, "AB")

        P = Prog(nc)
        P.mksem(ctx, "t", "v", "a", "cc", "sx", "ssm", "su8",
                "sc0", "sc1", "sc2", "scw", "wsa", "wsb",
                "w0a", "w0b", "w1a", "w1b", "w2a", "w2b",
                "sqw", "sga", "sgb", "swp", "aa0", "aa1",
                "gm0", "gm1", "grc", "gt0", "gt1", "go", "gdb", "av0", "av1")

        def bank(b, p0=0, p1=128, c0=0, c1=512):
            return PSF[p0:p1, b * 512 + c0:b * 512 + c1]

        def region(b, w=1024, p0=0, p1=128):
            return PSF[p0:p1, b * 512:b * 512 + w]

        psum_free = {}
        x_tok = {}

        o_x = af.alloc(NTILE * C)
        o_idn = af.alloc(128)
        o_sc = af.alloc(16)
        o_one8 = af.alloc(8)
        o_onef = af.alloc(128)
        o_bq = af.alloc(8)
        o_bk = af.alloc(8)
        o_lam = af.alloc(264)
        o_stat = af.alloc(64)
        o_xst = af.alloc(2 * C)
        o_wst = af.alloc(2 * C)
        o_m8 = af.alloc(8)
        o_mbuf = af.alloc(32)
        o_row = af.alloc(2048)
        o_rcp = o_row
        o_rbc = o_row + 1024

        def afs(off, p0, p1, c0, c1):
            return AFT[p0:p1, off + c0:off + c1]

        def abs_(off, p0, p1, c0, c1):
            return ABT[p0:p1, off + c0:off + c1]

        def xt(i):
            return afs(o_x, 0, 128, i * C, (i + 1) * C)

        IDN = afs(o_idn, 0, 128, 0, 128)
        S0D = afs(o_sc, 0, 128, 0, 1)
        S1R = afs(o_sc, 0, 128, 1, 2)
        S2H = afs(o_sc, 0, 128, 2, 3)
        LVN = afs(o_sc, 0, 128, 3, 4)
        SRAW = afs(o_sc, 0, 128, 4, 7)
        ONESF = afs(o_onef, 0, 1, 0, 128)
        ONES8 = afs(o_one8, 0, 8, 0, 1)

        o_cvt = ab.alloc(1024)
        o_cvt2 = ab.alloc(1024)
        o_oneb = ab.alloc(64)
        o_rb16 = ab.alloc(1024)
        o_bvb = ab.alloc(C)
        o_bpb = ab.alloc(C)
        CVT = abs_(o_cvt, 0, 128, 0, 1024)
        CVT2 = abs_(o_cvt2, 0, 128, 0, 1024)
        ONESB = abs_(o_oneb, 0, 1, 0, 64)
        RB16A = abs_(o_rb16, 0, 1, 0, 512)
        RB16B = abs_(o_rb16, 0, 1, 512, 1024)
        BVB = abs_(o_bvb, 0, 128, 0, C)
        BPB = abs_(o_bpb, 0, 128, 0, C)

        for i in range(NTILE):
            P.dma("y", "sx", xt(i), d_x[i * 128:(i + 1) * 128, :])
        t_x = ("sx", 16 * NTILE)
        for i in range(NTILE):
            x_tok[i] = t_x
        P.dma("y", "ssm", IDN, d_idn[:, :])
        P.dma("y", "ssm", SRAW, d_s[:, :])
        P.dma("y", "ssm", afs(o_bq, 0, 128, 0, 8), d_bqc[:, :])
        P.dma("y", "ssm", afs(o_bk, 0, 128, 0, 8), d_bkc[:, :])
        P.dma("y", "ssm", afs(o_row, 0, 1, 0, 2048), d_brow[:, :])
        P.dma("y", "ssm", afs(o_lam, 0, 8, 0, 256), d_lam[:, :])
        t_sm = ("ssm", 96)
        t_idn = t_s = t_bq = t_bk = t_row = t_lam = t_sm

        tv_s0 = P.op("v", lambda e: e.tensor_scalar_mul(S0D, SRAW[:, 0:1], 2.0),
                     waits=[t_s], inc="v")
        tv_s1 = P.op("v", lambda e: e.tensor_copy(out=S1R, in_=SRAW[:, 1:2]),
                     inc="v")
        tv_s2 = P.op("v", lambda e: e.tensor_scalar_mul(S2H, SRAW[:, 2:3], 0.5),
                     inc="v")
        tv_bq = P.op("v", lambda e: e.tensor_scalar_mul(
            afs(o_bq, 0, 128, 0, 8), afs(o_bq, 0, 128, 0, 8), 0.125),
            waits=[t_bq], inc="v")
        tv_onef = P.op("v", lambda e: e.memset(ONESF, 1.0), inc="v")
        tv_oneb = P.op("v", lambda e: e.memset(ONESB, 1.0), inc="v")
        tv_one8 = P.op("v", lambda e: e.memset(ONES8, 1.0), inc="v")

        te = None
        for ri, dst in ((0, BVB), (1, BPB)):
            for h in range(2):
                tmm = P.op("t", lambda e, ri=ri, h=h: e.matmul(
                    bank(7), ONESF,
                    afs(o_row, 0, 1, ri * C + h * 512, ri * C + (h + 1) * 512),
                    start=True, stop=True),
                    waits=[t_row, tv_onef, psum_free.get(7)], inc="t")
                te = P.op("a", lambda e, dst=dst, h=h: e.copy(
                    out=dst[:, h * 512:(h + 1) * 512], in_=bank(7)),
                    waits=[tmm], inc="a")
                psum_free[7] = te
        t_bias = te

        U8A = AUT[:, 0:1024]
        U8B = AUT[:, 1024:2048]
        lastu = None
        last_sc = None
        for ic in range(8):
            for g in range(4):
                ta = P.dma("y", "su8", U8A,
                           d_k0[ic * 128:(ic + 1) * 128, g * 1024:(g + 1) * 1024],
                           waits=[lastu])
                tb = P.dma("y", "su8", U8B,
                           d_k0[C + ic * 128:C + (ic + 1) * 128,
                                g * 1024:(g + 1) * 1024], waits=[lastu])
                lastu = P.op("v", lambda e: e.tensor_tensor(CVT, U8A, U8B,
                                                            ALU.subtract),
                             waits=[ta, tb, last_sc], inc="v")
                src = CVT.rearrange("p (h j) -> p h j", h=8)
                dst = s_w0d[g * 8:(g + 1) * 8, ic * 128:(ic + 1) * 128,
                            :].rearrange("h p j -> p h j")
                last_sc = P.dma("y", "sc0", dst, src, waits=[lastu])
        t_w0rdy = ("sc0", P.cnt["sc0"])
        for ic in range(32):
            for g in range(4):
                ta = P.dma("y", "su8", U8A,
                           d_k1[ic * 128:(ic + 1) * 128, g * 1024:(g + 1) * 1024],
                           waits=[lastu])
                lastu = P.op("v", lambda e: e.tensor_scalar(
                    CVT, U8A, 4.0, -2.0, ALU.mult, ALU.add),
                    waits=[ta, last_sc], inc="v")
                src = CVT.rearrange("p (h j) -> p h j", h=8)
                dst = s_w1x[g * 8:(g + 1) * 8, ic * 128:(ic + 1) * 128,
                            :].rearrange("h p j -> p h j")
                last_sc = P.dma("y", "sc1", dst, src, waits=[lastu])
        t_w1rdy = ("sc1", P.cnt["sc1"])
        for ic in range(32):
            ta = P.dma("y", "su8", U8A, d_k2[ic * 128:(ic + 1) * 128, :],
                       waits=[lastu])
            lastu = P.op("v", lambda e: e.tensor_scalar(
                CVT, U8A, 4.0, -2.0, ALU.mult, ALU.add),
                waits=[ta, last_sc], inc="v")
            last_sc = P.dma("y", "sc2", s_w2x[ic * 128:(ic + 1) * 128, :], CVT,
                            waits=[lastu])
        t_w2rdy = ("sc2", P.cnt["sc2"])
        wst_free = {}
        lastwc = None
        for wi, dwn in enumerate(("wqt", "wkt", "wvt", "wpt")):
            for ic in range(8):
                b = ic % 2
                st = afs(o_wst, 0, 128, b * C, (b + 1) * C)
                ta = P.dma("y", "wsa" if b == 0 else "wsb", st,
                           d_w[dwn][ic * 128:(ic + 1) * 128, :],
                           waits=[wst_free.get(b)])
                tcv = P.op("a", lambda e, st=st: e.copy(out=CVT2, in_=st),
                           waits=[ta, lastwc], inc="a")
                wst_free[b] = tcv
                lastwc = P.dma("y", "scw", s_wt[wi, ic * 128:(ic + 1) * 128, :],
                               CVT2, waits=[tcv])
        t_wtrdy = ("scw", P.cnt["scw"])

        stat_free = {}
        xst_free = {}
        fence = [None]
        TS_CUR = [TS]

        def ln_tile(ti, slot):
            st = o_stat + slot * 32
            stats = afs(st, 0, 128, 0, 12)
            mv = afs(st, 0, 128, 12, 14)
            vpe = afs(st, 0, 128, 14, 15)
            sd = afs(st, 0, 128, 15, 16)
            y0 = afs(st, 0, 128, 16, 17)
            t1 = afs(st, 0, 128, 17, 18)
            t2 = afs(st, 0, 128, 18, 19)
            t3 = afs(st, 0, 128, 19, 20)
            y1 = afs(st, 0, 128, 20, 21)
            nmr = afs(st, 0, 128, 21, 22)
            w = [x_tok[ti], stat_free.get(slot), fence[0]]
            P.op("v", lambda e: e.bn_stats(stats[:, 0:6], xt(ti)[:, 0:512]),
                 waits=w)
            tb = P.op("v", lambda e: e.bn_stats(stats[:, 6:12],
                                                xt(ti)[:, 512:1024]), inc="v")
            tg = P.op("v", lambda e: e.bn_aggr(mv, stats), waits=[tb], inc="v")
            tp = P.op("v", lambda e: e.tensor_scalar_add(vpe, mv[:, 1:2], EPS),
                      waits=[tg], inc="v")
            td = P.op("a", lambda e: e.activation(sd, vpe, AF.Sqrt), waits=[tp],
                      inc="a")
            t0 = P.op("v", lambda e: e.reciprocal(y0, sd), waits=[td], inc="v")
            ta_ = P.op("v", lambda e: e.tensor_mul(t1, y0, y0), waits=[t0],
                       inc="v")
            tb_ = P.op("v", lambda e: e.tensor_mul(t2, t1, vpe), waits=[ta_],
                       inc="v")
            tc_ = P.op("v", lambda e: e.tensor_scalar(t3, t2, -0.5, 1.5,
                                                      ALU.mult, ALU.add),
                       waits=[tb_], inc="v")
            td_ = P.op("v", lambda e: e.tensor_mul(y1, y0, t3), waits=[tc_],
                       inc="v")
            te_ = P.op("v", lambda e: e.scalar_tensor_tensor(
                nmr, mv[:, 0:1], -1.0, y1, ALU.mult, ALU.mult),
                waits=[td_], inc="v")
            tln = P.op("a", lambda e: e.activation(
                afs(o_xst, 0, 128, slot * C, (slot + 1) * C), xt(ti),
                AF.Identity, bias=nmr, scale=y1), waits=[te_, xst_free.get(slot)], inc="a")
            stat_free[slot] = tln
            return tln

        def ln_transpose(slot, o_dst, tokcol, tok_ln):
            tev = None
            for g in range(2):
                bk = 4 + g
                mm = None
                for u in range(4):
                    cc = g * 4 + u
                    mm = P.op("t", lambda e, slot=slot, cc=cc, bk=bk, u=u: e.transpose(
                        bank(bk, 0, 128, u * 128, (u + 1) * 128),
                        afs(o_xst, 0, 128, slot * C + cc * 128,
                            slot * C + (cc + 1) * 128), IDN),
                        waits=[tok_ln, t_idn, fence[0],
                               psum_free.get(bk) if u == 0 else None],
                        inc="t" if u == 3 else None)
                xst_free[slot] = mm
                base = o_dst + g * 4 * TS_CUR[0] + tokcol
                dst = ABT[:, base:base + 4 * TS_CUR[0]].rearrange(
                    "p (u c) -> p u c", c=TS_CUR[0])[:, :, 0:128]
                tev = P.op("a", lambda e, dst=dst, bk=bk: e.copy(
                    out=dst, in_=bank(bk)), waits=[mm], inc="a")
                psum_free[bk] = tev
            return tev

        def ffn_sublayer(entry=()):
            ab.push()
            o_xlnT = ab.alloc(8 * TS)
            o_h1 = ab.alloc(32 * TS)
            o_w0b = ab.alloc(2 * 1024)
            o_w1b = ab.alloc(2 * 4096)
            o_w2b = ab.alloc(2 * 1024)
            TS_CUR[0] = TS

            def h1_ap(i):
                return abs_(o_h1, 0, 128, i * TS, (i + 1) * TS)

            def h2_ap(i):
                return abs_(o_xlnT, 0, 128, i * TS, (i + 1) * TS)

            def xlnT_ap(ic, c0, c1):
                return abs_(o_xlnT, 0, 128, ic * TS + c0, ic * TS + c1)

            w0_free = {}
            w1_free = {}
            w2_free = {}
            prev_done = [None]
            mm_last = {}

            for ts in range(2):
                fence[0] = ("t", P.cnt["t"])
                t_xlnT = None
                for j in range(8):
                    ti = ts * 8 + j
                    tok_ln = ln_tile(ti, j % 2)
                    t_xlnT = ln_transpose(j % 2, o_xlnT, j * 128, tok_ln)

                h1_toks = {}
                for h1c in range(32):
                    wb = h1c % 2
                    wtok = P.dma("y", "w0a" if wb == 0 else "w0b",
                                 abs_(o_w0b, 0, 128, wb * 1024,
                                      (wb + 1) * 1024),
                                 s_w0d[h1c].rearrange("(a p) j -> p a j", p=128),
                                 waits=[t_w0rdy, w0_free.get(wb)] + list(entry))
                    rg = h1c % 2
                    mm = None
                    for half in range(2):
                        for icc in range(8):
                            mm = P.op("t", lambda e, half=half, icc=icc, wb=wb, rg=rg: e.matmul(
                                bank(rg * 2 + half),
                                abs_(o_w0b, 0, 128, wb * 1024 + icc * 128,
                                     wb * 1024 + (icc + 1) * 128),
                                xlnT_ap(icc, half * 512, (half + 1) * 512),
                                start=(icc == 0), stop=(icc == 7)),
                                waits=[wtok, t_xlnT,
                                       psum_free.get(rg * 2 + half) if icc == 0 else None],
                                inc="t" if (half == 1 and icc == 7) else None)
                    w0_free[wb] = mm
                    tev = P.op("a", lambda e, h1c=h1c, rg=rg: e.activation(
                        h1_ap(h1c), region(rg * 2), AF.Relu, scale=S0D),
                        waits=[mm, tv_s0,
                               prev_done[0] if h1c < 2 else None], inc="a")
                    psum_free[rg * 2] = tev
                    psum_free[rg * 2 + 1] = tev
                    h1_toks[h1c] = tev

                t_m1mm = ("t", P.cnt["t"])
                for Q in range(4):
                    h2_toks = {}
                    for idx in range(8):
                        h2c = Q * 8 + idx
                        wb = idx % 2
                        wtok = P.dma("y",
                                     "w1a" if wb == 0 else "w1b",
                                     abs_(o_w1b, 0, 128, wb * 4096,
                                          (wb + 1) * 4096),
                                     s_w1x[h2c].rearrange("(a p) j -> p a j",
                                                          p=128),
                                     waits=[t_w1rdy, w1_free.get(wb)] + list(entry))
                        rg = idx % 2
                        mm = None
                        for half in range(2):
                            for h1c in range(32):
                                mm = P.op("t", lambda e, half=half, h1c=h1c, wb=wb, rg=rg: e.matmul(
                                    bank(rg * 2 + half),
                                    abs_(o_w1b, 0, 128, wb * 4096 + h1c * 128,
                                         wb * 4096 + (h1c + 1) * 128),
                                    h1_ap(h1c)[:, half * 512:(half + 1) * 512],
                                    start=(h1c == 0), stop=(h1c == 31)),
                                    waits=[wtok, h1_toks[31],
                                           psum_free.get(rg * 2 + half) if h1c == 0 else None],
                                    inc="t" if (half == 1 and h1c == 31) else None)
                        w1_free[wb] = mm
                        tev = P.op("a", lambda e, idx=idx, rg=rg: e.activation(
                            h2_ap(idx), region(rg * 2), AF.Relu, scale=S1R),
                            waits=[mm, tv_s1, t_m1mm], inc="a")
                        psum_free[rg * 2] = tev
                        psum_free[rg * 2 + 1] = tev
                        h2_toks[idx] = tev

                    for tcg in range(2):
                        mm_last = {}
                        for idx in range(8):
                            h2c = Q * 8 + idx
                            wb = idx % 2
                            wtok = P.dma("y", "w2a" if wb == 0 else "w2b",
                                         abs_(o_w2b, 0, 128, wb * 1024,
                                              (wb + 1) * 1024),
                                         s_w2x[h2c * 128:(h2c + 1) * 128, :],
                                         waits=[t_w2rdy, w2_free.get(wb)] + list(entry))
                            for u in range(4):
                                tc = tcg * 4 + u
                                for half in range(2):
                                    bk = u * 2 + half
                                    mm_last[(u, half)] = P.op(
                                        "t", lambda e, idx=idx, tc=tc, half=half, wb=wb, bk=bk: e.matmul(
                                            bank(bk),
                                            h2_ap(idx)[:, tc * 128:(tc + 1) * 128],
                                            abs_(o_w2b, 0, 128,
                                                 wb * 1024 + half * 512,
                                                 wb * 1024 + (half + 1) * 512),
                                            start=(idx == 0), stop=(idx == 7)),
                                        waits=[wtok, h2_toks[idx],
                                               psum_free.get(bk) if idx == 0 else None],
                                        inc="t" if (u == 3 and half == 1) or idx == 7 else None)
                            w2_free[wb] = mm_last[(3, 1)]
                        for u in range(4):
                            tc = tcg * 4 + u
                            ti = ts * 8 + tc
                            for half in range(2):
                                bk = u * 2 + half
                                tev = P.op("v", lambda e, ti=ti, half=half, bk=bk: e.scalar_tensor_tensor(
                                    xt(ti)[:, half * 512:(half + 1) * 512],
                                    bank(bk), S2H,
                                    xt(ti)[:, half * 512:(half + 1) * 512],
                                    ALU.mult, ALU.add),
                                    waits=[mm_last[(u, half)], tv_s2,
                                           x_tok[ti]],
                                    inc="v")
                                psum_free[bk] = tev
                                x_tok[ti] = tev
                prev_done[0] = mm_last[(3, 1)]
            ab.pop()

        def emit_qkv(entry=()):
            ab.push()
            o_xl2 = ab.alloc(8 * T)
            o_wq4 = [ab.alloc(8 * 1024) for _ in range(3)]
            o_stg = ab.alloc(2 * T)
            TS_CUR[0] = T
            fence[0] = ("t", P.cnt["t"])

            def xl2_ap(ic, c0, c1):
                return abs_(o_xl2, 0, 128, ic * T + c0, ic * T + c1)

            t_xl2 = None
            for ti in range(16):
                tok_ln = ln_tile(ti, ti % 2)
                t_xl2 = ln_transpose(ti % 2, o_xl2, ti * 128, tok_ln)

            wt_toks = []
            for wi in range(3):
                for ic in range(8):
                    wt_toks.append(P.dma(
                        "y", "sqw",
                        abs_(o_wq4[wi], 0, 128, ic * 1024, (ic + 1) * 1024),
                        s_wt[wi, ic * 128:(ic + 1) * 128, :],
                        waits=[t_wtrdy] + list(entry)))
            t_wload = ("sqw", P.cnt["sqw"])

            stg_free = {}
            for which in range(2):
                dst_dram = s_q if which == 0 else s_kgi
                bias = afs(o_bq if which == 0 else o_bk, 0, 128, 0, 8)
                scale = 0.125 if which == 0 else 1.0
                bias_tok = tv_bq if which == 0 else t_bk
                for oc in range(8):
                    slot = oc % 2
                    evs = []
                    for nt in range(4):
                        bk = (oc * 4 + nt) % 4
                        mm = None
                        for icc in range(8):
                            mm = P.op("t", lambda e, which=which, oc=oc, nt=nt, icc=icc, bk=bk: e.matmul(
                                bank(bk),
                                abs_(o_wq4[which], 0, 128,
                                     icc * 1024 + oc * 128,
                                     icc * 1024 + (oc + 1) * 128),
                                xl2_ap(icc, nt * 512, (nt + 1) * 512),
                                start=(icc == 0), stop=(icc == 7)),
                                waits=[t_wload, t_xl2,
                                       psum_free.get(bk) if icc == 0 else None],
                                inc="t" if icc == 7 else None)
                        tev = P.op("a", lambda e, slot=slot, nt=nt, bk=bk, bias=bias, oc=oc, scale=scale: e.activation(
                            abs_(o_stg, 0, 128, slot * T + nt * 512,
                                 slot * T + (nt + 1) * 512),
                            bank(bk), AF.Identity, bias=bias[:, oc:oc + 1],
                            scale=scale),
                            waits=[mm, bias_tok, stg_free.get(slot)], inc="a")
                        psum_free[bk] = tev
                        evs.append(tev)
                    stg_free[slot] = P.dma(
                        "y", "sga" if slot == 0 else "sgb",
                        dst_dram[oc * 128:(oc + 1) * 128, :],
                        abs_(o_stg, 0, 128, slot * T, (slot + 1) * T),
                        waits=evs)
            for tc in range(16):
                slot = tc % 2
                rg = tc % 2
                mm = None
                for icc in range(8):
                    for half in range(2):
                        mm = P.op("t", lambda e, tc=tc, icc=icc, half=half, rg=rg: e.matmul(
                            bank(rg * 2 + half),
                            xl2_ap(icc, tc * 128, (tc + 1) * 128),
                            abs_(o_wq4[2], 0, 128, icc * 1024 + half * 512,
                                 icc * 1024 + (half + 1) * 512),
                            start=(icc == 0), stop=(icc == 7)),
                            waits=[t_wload, t_xl2,
                                   psum_free.get(rg * 2 + half) if icc == 0 else None],
                            inc="t" if (icc == 7 and half == 1) else None)
                tev = P.op("v", lambda e, slot=slot, rg=rg: e.tensor_tensor(
                    abs_(o_stg, 0, 128, slot * T, slot * T + 1024),
                    region(rg * 2), BVB, ALU.add),
                    waits=[mm, t_bias, stg_free.get(slot)], inc="v")
                psum_free[rg * 2] = tev
                psum_free[rg * 2 + 1] = tev
                stg_free[slot] = P.dma(
                    "y", "sga" if slot == 0 else "sgb",
                    s_vgi[tc * 128:(tc + 1) * 128, :],
                    abs_(o_stg, 0, 128, slot * T, slot * T + 1024), waits=[tev])

            t_ydall = [("sga", P.cnt["sga"]), ("sgb", P.cnt["sgb"])]
            rg_pairs = [[0, 1], [2, 3], [4, 5], [6, 7]]
            tcc_k = P.op("g", lambda e: e.collective_compute(
                "AllGather", ALU.bypass, replica_groups=rg_pairs,
                ins=[s_kgi.ap().opt()], outs=[s_kgo.ap().opt()]),
                waits=t_ydall, inc="cc")
            tcc_v = P.op("g", lambda e: e.collective_compute(
                "AllGather", ALU.bypass, replica_groups=rg_pairs,
                ins=[s_vgi.ap().opt()], outs=[s_vgo.ap().opt()]),
                waits=[], inc="cc")
            if dbg:
                P.dma("g", "gdb", d_dbgq[:, :], s_q[:, :],
                      waits=t_ydall + [tcc_k, tcc_v])
                P.dma("g", "gdb", d_dbgk[:, :], s_kgo[:, :],
                      waits=[tcc_k, tcc_v])
                P.dma("g", "gdb", d_dbgv[:, :], s_vgo[:, :],
                      waits=[tcc_k, tcc_v])
            ab.pop()
            return (tcc_k, tcc_v)

        ngm_free = {}
        pt_free = {}
        tmb_free = {}
        rcp_free = {}

        def emit_attention(tcc):
            ab.push()
            o_set = []
            for s in range(2):
                o_set.append(dict(qa=ab.alloc(T), qb=ab.alloc(T),
                                  ka=ab.alloc(2 * T), kb=ab.alloc(2 * T),
                                  vp=ab.alloc(64 * 65)))
            o_pt = ab.alloc(2 * 1024)
            o_afm = ab.alloc(2 * 2 * T)
            o_wpm = ab.alloc(8 * 1024)
            o_tmb = ab.alloc(512)
            o_ngm = ab.alloc(128)
            fence[0] = ("t", P.cnt["t"])

            for pc in range(8):
                P.dma("y", "swp",
                      abs_(o_wpm, 0, 128, pc * 1024, (pc + 1) * 1024),
                      s_wt[3, pc * 128:(pc + 1) * 128, :],
                      waits=[t_wtrdy, ("cc", 2)])
            t_wpl = ("swp", 128)
            lq1 = afs(o_lam, 0, 8, 0, 64)
            lk1 = afs(o_lam, 0, 8, 64, 128)
            lq2 = afs(o_lam, 0, 8, 128, 192)
            lk2 = afs(o_lam, 0, 8, 192, 256)
            l1c = afs(o_lam, 0, 8, 256, 257)
            l2c = afs(o_lam, 0, 8, 257, 258)
            ddc = afs(o_lam, 0, 8, 258, 259)
            lvs = afs(o_lam, 0, 1, 259, 260)
            tk = P.op("v", lambda e: e.tensor_mul(lq1, lq1, lk1), waits=[t_lam],
                      inc="v")
            tk = P.op("v", lambda e: e.tensor_reduce(l1c, lq1, AX.X, ALU.add),
                      waits=[tk], inc="v")
            tk = P.op("v", lambda e: e.tensor_scalar_min(l1c, l1c, 5.0),
                      waits=[tk], inc="v")
            tk1 = P.op("a", lambda e: e.activation(l1c, l1c, AF.Exp),
                       waits=[tk], inc="a")
            tk = P.op("v", lambda e: e.tensor_mul(lq2, lq2, lk2), waits=[t_lam],
                      inc="v")
            tk = P.op("v", lambda e: e.tensor_reduce(l2c, lq2, AX.X, ALU.add),
                      waits=[tk], inc="v")
            tk = P.op("v", lambda e: e.tensor_scalar_min(l2c, l2c, 5.0),
                      waits=[tk], inc="v")
            tk2 = P.op("a", lambda e: e.activation(l2c, l2c, AF.Exp),
                       waits=[tk], inc="a")
            tk = P.op("v", lambda e: e.tensor_sub(ddc, l1c, l2c),
                      waits=[tk1, tk2], inc="v")
            tmm = P.op("t", lambda e: e.matmul(bank(6, 0, 1, 0, 1), ddc, ONES8,
                                               start=True, stop=True),
                       waits=[tk, tv_one8, psum_free.get(6)], inc="t")
            tlv = P.op("a", lambda e: e.activation(lvs, bank(6, 0, 1, 0, 1),
                                                   AF.Copy, bias=-0.8,
                                                   scale=-0.125),
                       waits=[tmm], inc="a")
            tmm = P.op("t", lambda e: e.matmul(bank(6, 0, 128, 0, 1), ONESF,
                                               lvs, start=True, stop=True),
                       waits=[tlv, tv_onef], inc="t")
            tlvn = P.op("v", lambda e: e.tensor_copy(
                out=LVN, in_=bank(6, 0, 128, 0, 1)), waits=[tmm], inc="v")
            psum_free[6] = tlvn
            t_wpm = None
            for pc in range(4):
                t_wpm = P.op("v", lambda e, pc=pc: e.scalar_tensor_tensor(
                    abs_(o_wpm, 0, 128, (4 + pc) * 1024, (5 + pc) * 1024),
                    abs_(o_wpm, 0, 128, pc * 1024, (pc + 1) * 1024), LVN,
                    abs_(o_wpm, 0, 128, (4 + pc) * 1024, (5 + pc) * 1024),
                    ALU.mult, ALU.add),
                    waits=[tlvn, t_wpl], inc="v")

            set_free = {}
            afm_free = {}
            for p in range(8):
                s = p % 2
                oS = o_set[s]
                bfree = set_free.get(s)
                asem = "aa0" if s == 0 else "aa1"
                tqa = P.dma("y", asem, abs_(oS["qa"], 0, 64, 0, T),
                            s_q[128 * p:128 * p + 64, :],
                            waits=[bfree, ("cc", 2)])
                tqb = P.dma("y", asem, abs_(oS["qb"], 0, 64, 0, T),
                            s_q[128 * p + 64:128 * p + 128, :], waits=[bfree])
                tka0 = P.dma("y", asem, abs_(oS["ka"], 0, 64, 0, T),
                             s_kgo[128 * p:128 * p + 64, :], waits=[bfree])
                tka1 = P.dma("y", asem, abs_(oS["ka"], 0, 64, T, 2 * T),
                             s_kgo[C + 128 * p:C + 128 * p + 64, :],
                             waits=[bfree])
                tkb0 = P.dma("y", asem, abs_(oS["kb"], 0, 64, 0, T),
                             s_kgo[128 * p + 64:128 * p + 128, :],
                             waits=[bfree])
                tkb1 = P.dma("y", asem, abs_(oS["kb"], 0, 64, T, 2 * T),
                             s_kgo[C + 128 * p + 64:C + 128 * p + 128, :],
                             waits=[bfree])
                tone_a = P.op("v", lambda e, oS=oS: e.memset(
                    abs_(oS["ka"], 64, 65, 0, 2 * T), 1.0), waits=[bfree],
                    inc="v")
                tone_b = P.op("v", lambda e, oS=oS: e.memset(
                    abs_(oS["kb"], 64, 65, 0, 2 * T), 1.0), waits=[bfree],
                    inc="v")
                vp_toks = []
                for hh in range(2):
                    cbase = 128 * p + hh * 64
                    for g in range(4):
                        dst = ABT[0:128,
                                  oS["vp"] + hh * 2080 + g * 8 * 65:
                                  oS["vp"] + hh * 2080 + (g + 1) * 8 * 65].rearrange(
                            "p (j d) -> p j d", d=65)[:, :, 0:64]
                        srcv = s_vgo[g * 8 * 128:(g + 1) * 8 * 128,
                                     cbase:cbase + 64].rearrange(
                            "(j p) d -> p j d", p=128)
                        vp_toks.append(P.dma("y", "av0" if s == 0 else "av1",
                                             dst, srcv,
                                             waits=[bfree, ("cc", 2)]))
                tone_v = P.op("v", lambda e, oS=oS: e.memset(
                    ABT[0:128, oS["vp"]:oS["vp"] + 64 * 65].rearrange(
                        "p (j d) -> p j d", d=65)[:, :, 64:65], 1.0),
                    waits=[bfree], inc="v")
                t_load = (asem, P.cnt[asem])
                t_vload = ("av0" if s == 0 else "av1",
                           P.cnt["av0" if s == 0 else "av1"])
                tqa = tqb = tka0 = tka1 = tkb0 = tkb1 = t_load
                vp_toks = [t_load, t_vload]

                def qa(c0, c1, oS=oS):
                    return abs_(oS["qa"], 0, 64, c0, c1)

                def qb(c0, c1, oS=oS):
                    return abs_(oS["qb"], 0, 64, c0, c1)

                def ka(c0, c1, oS=oS):
                    return abs_(oS["ka"], 0, 64, c0, c1)

                def kb(c0, c1, oS=oS):
                    return abs_(oS["kb"], 0, 64, c0, c1)

                mbuf = afs(o_mbuf, 0, 128, 0, 32)
                m8 = afs(o_m8, 0, 128, 0, 8)
                mm_tok = None
                tr = None
                heads = [(qa, ka), (qb, kb)]
                for hi in range(2):
                    qh, kh = heads[hi]
                    for qt in range(16):
                        for kc in range(8):
                            bk = kc % 2
                            mma = P.op("t", lambda e, qh=qh, kh=kh, qt=qt, kc=kc, bk=bk: e.matmul(
                                bank(bk), qh(qt * 128, (qt + 1) * 128),
                                kh(kc * 512, (kc + 1) * 512),
                                start=True, stop=True),
                                waits=[t_load, psum_free.get(bk)], inc="t")
                            tr = P.op("v", lambda e, kc=kc, bk=bk: e.tensor_reduce(
                                m8[:, kc:kc + 1], bank(bk), AX.X, ALU.max),
                                waits=[mma,
                                       ngm_free.get(0) if (hi == 0 and qt == 0 and kc == 0) else None,
                                       mm_tok if kc == 0 else None],
                                inc="v")
                            psum_free[bk] = tr
                        mm_tok = P.op("v", lambda e, hi=hi, qt=qt: e.tensor_reduce(
                            mbuf[:, hi * 16 + qt:hi * 16 + qt + 1], m8, AX.X,
                            ALU.max), waits=[tr], inc="v")
                tmt = P.op("t", lambda e, mbuf=mbuf: e.transpose(
                    bank(2, 0, 32, 0, 128), mbuf, IDN),
                    waits=[mm_tok, psum_free.get(2), t_idn], inc="t")
                tng = P.op("a", lambda e: e.activation(
                    abs_(o_ngm, 0, 32, 0, 128), bank(2, 0, 32, 0, 128),
                    AF.Copy, scale=-1.0),
                    waits=[tmt, ngm_free.get(0)], inc="a")
                psum_free[2] = tng
                dstA = ABT[64:65, oS["qa"]:oS["qa"] + T]
                dstB = ABT[64:65, oS["qb"]:oS["qb"] + T]
                gmsem = "gm0" if s == 0 else "gm1"
                tmA = P.dma("g", gmsem, dstA, abs_(o_ngm, 0, 16, 0, 128),
                            waits=[tng, bfree])
                tmB = P.dma("g", gmsem, dstB, abs_(o_ngm, 16, 32, 0, 128),
                            waits=[tng, bfree])
                tmA = tmB = (gmsem, P.cnt[gmsem])
                ngm_free[0] = tmB

                def kaf(c0, c1, oS=oS):
                    return abs_(oS["ka"], 0, 65, c0, c1)

                def kbf(c0, c1, oS=oS):
                    return abs_(oS["kb"], 0, 65, c0, c1)

                def qaf(c0, c1, oS=oS):
                    return abs_(oS["qa"], 0, 65, c0, c1)

                def qbf(c0, c1, oS=oS):
                    return abs_(oS["qb"], 0, 65, c0, c1)

                grp = p // 2
                gslot = grp % 2
                o_af = o_afm + gslot * 2 * T + (p % 2) * T
                pv_last = {}
                for qg in range(4):
                    for hi in range(2):
                        kf_ = kaf if hi == 0 else kbf
                        qf_ = qaf if hi == 0 else qbf
                        vbase = oS["vp"] + hi * 2080
                        pvbank = 6 + hi
                        pvl = None
                        for kreg in range(16):
                            rb = 2 + (kreg % 2) * 2
                            mm = None
                            for jj in range(2):
                                j = kreg * 2 + jj
                                mm = P.op("t", lambda e, j=j, jj=jj, rb=rb, qg=qg, kf_=kf_, qf_=qf_: e.matmul(
                                    bank(rb + jj),
                                    kf_(j * 128, (j + 1) * 128),
                                    qf_(qg * 512, (qg + 1) * 512),
                                    start=True, stop=True),
                                    waits=[tmA, tone_a, tone_b,
                                           psum_free.get(rb + jj)],
                                    inc="t" if jj == 1 else None)
                            pr = kreg % 2
                            tex = P.op("a", lambda e, pr=pr, rb=rb: e.activation(
                                abs_(o_pt, 0, 128, pr * 1024, (pr + 1) * 1024),
                                region(rb), AF.Exp),
                                waits=[mm, pt_free.get(pr)], inc="a")
                            psum_free[rb] = tex
                            psum_free[rb + 1] = tex
                            for jj in range(2):
                                j = kreg * 2 + jj
                                pvl = P.op("t", lambda e, j=j, jj=jj, pr=pr, vbase=vbase, pvbank=pvbank, kreg=kreg: e.matmul(
                                    bank(pvbank, 0, 65),
                                    ABT[0:128, vbase + j * 65:vbase + (j + 1) * 65],
                                    abs_(o_pt, 0, 128, pr * 1024 + jj * 512,
                                         pr * 1024 + (jj + 1) * 512),
                                    start=(kreg == 0 and jj == 0),
                                    stop=(kreg == 15 and jj == 1)),
                                    waits=[tex, tone_v, vp_toks[-1]] +
                                          ([psum_free.get(pvbank)] if (kreg == 0 and jj == 0) else []),
                                    inc="t" if jj == 1 else None)
                                pt_free[pr] = pvl
                        pv_last[hi] = pvl
                    trA = P.op("v", lambda e: e.reciprocal(
                        afs(o_rcp, 64, 65, 0, 512), bank(6, 64, 65)),
                        waits=[pv_last[0], rcp_free.get(0)], inc="v")
                    trB = P.op("v", lambda e: e.reciprocal(
                        afs(o_rcp, 64, 65, 512, 1024), bank(7, 64, 65)),
                        waits=[pv_last[1]], inc="v")
                    tshp = P.dma("g", "grc", afs(o_rcp, 0, 1, 0, 1024),
                                 afs(o_rcp, 64, 65, 0, 1024),
                                 waits=[trA, trB])
                    tcA = P.op("v", lambda e: e.tensor_copy(
                        out=RB16A, in_=afs(o_rcp, 0, 1, 0, 512)),
                        waits=[tshp], inc="v")
                    tcB = P.op("v", lambda e: e.tensor_copy(
                        out=RB16B, in_=afs(o_rcp, 0, 1, 512, 1024)),
                        waits=[tshp], inc="v")
                    rcp_free[0] = tcB
                    tm1 = P.op("t", lambda e: e.matmul(
                        bank(2, 0, 64), ONESB, RB16A, start=True, stop=True),
                        waits=[tcA, tv_oneb, psum_free.get(2)], inc="t")
                    tm2 = P.op("t", lambda e: e.matmul(
                        bank(3, 0, 64), ONESB, RB16B, start=True, stop=True),
                        waits=[tcB, psum_free.get(3)], inc="t")
                    trb = P.op("a", lambda e: e.copy(
                        out=afs(o_rbc, 0, 64, 0, 1024),
                        in_=region(2, 1024, 0, 64)),
                        waits=[tm1, tm2], inc="a")
                    psum_free[2] = trb
                    psum_free[3] = trb
                    evA = P.op("v", lambda e, o_af=o_af, qg=qg: e.tensor_tensor(
                        abs_(o_af, 0, 64, qg * 512, (qg + 1) * 512),
                        bank(6, 0, 64), afs(o_rbc, 0, 64, 0, 512), ALU.mult),
                        waits=[trb,
                               afm_free.get(gslot) if qg == 0 else None],
                        inc="v")
                    evB = P.op("v", lambda e: e.tensor_tensor(
                        abs_(o_tmb, 0, 64, 0, 512),
                        bank(7, 0, 64), afs(o_rbc, 0, 64, 512, 1024),
                        ALU.mult),
                        waits=[trb, tmb_free.get(0)], inc="v")
                    psum_free[6] = evA
                    psum_free[7] = evB
                    tsh = P.dma("g", "gt0" if gslot == 0 else "gt1",
                                abs_(o_af, 64, 128, qg * 512, (qg + 1) * 512),
                                abs_(o_tmb, 0, 64, 0, 512), waits=[evB])
                    tmb_free[0] = tsh
                set_free[s] = pvl

                if p == 1 and dbg == "afm":
                    P.dma("g", "gdb", d_dbga[:, :],
                          abs_(o_afm, 0, 128, 0, 2 * T),
                          waits=[("v", P.cnt["v"]),
                                 ("gt0", P.cnt["gt0"])])
                if p % 2 == 1:
                    t_grp = [("v", P.cnt["v"]),
                             ("gt0" if gslot == 0 else "gt1",
                              P.cnt["gt0" if gslot == 0 else "gt1"])]
                    proj_last = None
                    for tc in range(16):
                        rg = tc % 2
                        mm = None
                        for pc2 in range(2):
                            pcw = grp * 2 + pc2
                            for half in range(2):
                                mm = P.op("t", lambda e, tc=tc, pc2=pc2, pcw=pcw, half=half, rg=rg, gslot=gslot: e.matmul(
                                    bank(rg * 2, 0, 128, half * 512,
                                         (half + 1) * 512),
                                    abs_(o_afm + gslot * 2 * T + pc2 * T,
                                         0, 128, tc * 128, (tc + 1) * 128),
                                    abs_(o_wpm, 0, 128,
                                         pcw * 1024 + half * 512,
                                         pcw * 1024 + (half + 1) * 512),
                                    start=(pc2 == 0), stop=(pc2 == 1)),
                                    waits=t_grp + [t_wpm,
                                                   (psum_free.get(rg * 2) if half == 0 else psum_free.get(rg * 2 + 1)) if pc2 == 0 else None],
                                    inc="t" if (pc2 == 1 and half == 1) else None)
                        tv1 = P.op("v", lambda e, tc=tc, rg=rg: e.tensor_tensor(
                            xt(tc), region(rg * 2), xt(tc), ALU.add),
                            waits=[mm, x_tok[tc]], inc="v")
                        psum_free[rg * 2] = tv1
                        psum_free[rg * 2 + 1] = tv1
                        x_tok[tc] = tv1
                        proj_last = tv1
                    afm_free[gslot] = proj_last
            for tc in range(16):
                tvb = P.op("v", lambda e, tc=tc: e.tensor_tensor(
                    xt(tc), xt(tc), BPB, ALU.add),
                    waits=[x_tok[tc], t_bias], inc="v")
                x_tok[tc] = tvb
            ab.pop()

        ffn_sublayer()
        if stage in ("qkv", "x2", "full"):
            tcc = emit_qkv(entry=[("t", P.cnt["t"])])
        if stage in ("x2", "full"):
            emit_attention(tcc)
        if stage == "full":
            t_att = [("t", P.cnt["t"]), ("v", P.cnt["v"]), ("a", P.cnt["a"]),
                     ("gt0", P.cnt["gt0"]), ("gt1", P.cnt["gt1"]),
                     ("grc", P.cnt["grc"]),
                     ("gm0", P.cnt["gm0"]), ("gm1", P.cnt["gm1"])]
            ffn_sublayer(entry=t_att)

        for i in range(NTILE):
            P.dma("g", "go", d_out[i * 128:(i + 1) * 128, :], xt(i),
                  waits=[x_tok[i]])
        final_waits = [("go", P.cnt["go"])]
        if dbg:
            final_waits.append(("gdb", P.cnt["gdb"]))
        P.op("g", lambda e: e.nop(), waits=final_waits)

        print("[build] ops: " + " ".join(f"{e}:{len(P.q[e])}" for e in ENGS) +
              f"  AF hwm {af.hwm}/{NF}  AB hwm {ab.hwm}/{NB}")

        with nc.Block() as block:
            @block.tensor
            def _(eng):
                for f in P.q["t"]:
                    f(eng)

            @block.vector
            def _(eng):
                for f in P.q["v"]:
                    f(eng)

            @block.scalar
            def _(eng):
                for f in P.q["a"]:
                    f(eng)

            @block.gpsimd
            def _(eng):
                for f in P.q["g"]:
                    f(eng)

            @block.sync
            def _(eng):
                for f in P.q["y"]:
                    f(eng)

    return nc


def host_inputs(inputs, core):
    b, h = core // 2, core % 2
    f32 = np.float32
    x = np.ascontiguousarray(
        np.asarray(inputs["L0_attn_x"], dtype=f32)[b, h * T:(h + 1) * T])
    return {
        "x_in": x,
        "wqt": np.ascontiguousarray(np.asarray(inputs["L0_attn_Wq"], f32).T),
        "wkt": np.ascontiguousarray(np.asarray(inputs["L0_attn_Wk"], f32).T),
        "wvt": np.ascontiguousarray(np.asarray(inputs["L0_attn_Wv"], f32).T),
        "wpt": np.ascontiguousarray(np.asarray(inputs["L0_attn_Wp"], f32).T),
        "bq_c": np.ascontiguousarray(
            np.asarray(inputs["L0_attn_bq"], f32).reshape(8, 128).T),
        "bk_c": np.ascontiguousarray(
            np.asarray(inputs["L0_attn_bk"], f32).reshape(8, 128).T),
        "b_row": np.ascontiguousarray(np.stack(
            [np.asarray(inputs["L0_attn_bv"], f32),
             np.asarray(inputs["L0_attn_bp"], f32)])),
        "lam": np.ascontiguousarray(np.concatenate(
            [np.asarray(inputs["L0_attn_lambda_q1"], f32).reshape(8, 64),
             np.asarray(inputs["L0_attn_lambda_k1"], f32).reshape(8, 64),
             np.asarray(inputs["L0_attn_lambda_q2"], f32).reshape(8, 64),
             np.asarray(inputs["L0_attn_lambda_k2"], f32).reshape(8, 64)],
            axis=1)),
        "k0": np.ascontiguousarray(np.asarray(inputs["L0_ffn_k0"]).view(np.uint8)),
        "k1": np.ascontiguousarray(np.asarray(inputs["L0_ffn_k1"]).view(np.uint8)),
        "k2": np.ascontiguousarray(np.asarray(inputs["L0_ffn_k2"]).view(np.uint8)),
        "s_r": np.ascontiguousarray(np.broadcast_to(np.stack(
            [np.asarray(inputs["L0_ffn_s0"], f32).reshape(()),
             np.asarray(inputs["L0_ffn_s1"], f32).reshape(()),
             np.asarray(inputs["L0_ffn_s2"], f32).reshape(())])[None, :],
            (128, 3))),
        "idn": np.eye(128, dtype=f32),
    }


def kernel(**inputs):
    nc = build_program("full")
    in_maps = [host_inputs(inputs, c) for c in range(8)]
    res = run_bass_kernel_spmd(nc, in_maps, core_ids=list(range(8)))
    out = np.empty((4, 4096, 1024), dtype=np.float32)
    for c in range(8):
        b, h = c // 2, c % 2
        out[b, h * T:(h + 1) * T] = res.results[c]["out"]
    return out
